# version 23
# baseline (speedup 1.0000x reference)
import numpy as np
import concourse.bass as bass
import concourse.mybir as mybir
from concourse.bass_utils import run_bass_kernel_spmd

import ml_dtypes
BFNP = ml_dtypes.bfloat16

F32 = mybir.dt.float32
BF16 = mybir.dt.bfloat16
AF = mybir.ActivationFunctionType
ALU = mybir.AluOpType
AX = mybir.AxisListType
f32 = np.float32

SEQ = 8192
DM = 2048
NC_ = 8
TOKS = SEQ // NC_
NWIN = 16
WIN = 512
NQB = 16
QB = 512
CEXP = f32(5.0)
SSCL = f32(1.0 / np.sqrt(128.0))
VW = 132
NKC = 64


def _sweeps():
    out = []
    pb = 0
    for qb in range(NQB):
        for h in range(2):
            npair = 2 * (qb + 1)
            out.append((h, qb, npair, pb))
            pb += npair
    return out


SWEEPS = _sweeps()


def build_nc():
    nc = bass.Bass("TRN2", target_bir_lowering=False, debug=False, num_devices=NC_,
                   detect_race_conditions=True)

    xq = nc.dram_tensor("xq", [NWIN, 16, 128, WIN], F32, kind="ExternalInput").ap()
    wq = nc.dram_tensor("wq", [128, 16 * 768], F32, kind="ExternalInput").ap()
    wo = nc.dram_tensor("wo", [128, 16 * 2048], F32, kind="ExternalInput").ap()
    wgT = nc.dram_tensor("wgT", [32, 16, 128, 256], F32, kind="ExternalInput").ap()
    wuT = nc.dram_tensor("wuT", [32, 16, 128, 256], F32, kind="ExternalInput").ap()
    wdT = nc.dram_tensor("wdT", [2, 8, 8, 128, 1024], F32, kind="ExternalInput").ap()
    xres = nc.dram_tensor("xres", [TOKS, DM], F32, kind="ExternalInput").ap()
    out = nc.dram_tensor("out", [TOKS, DM], F32, kind="ExternalOutput").ap()

    ag_in = nc.dram_tensor("ag_in", [SEQ, 256], BF16, kind="Internal").ap()
    ag_out = nc.dram_tensor("ag_out", [NC_, NC_ * TOKS, 256], BF16, kind="Internal").ap()
    wo_b = nc.dram_tensor("wo_b", [128, 16 * 2048], BF16, kind="Internal").ap()
    wdT_b = nc.dram_tensor("wdT_b", [2, 8, 8, 128, 1024], BF16, kind="Internal").ap()
    attn_my = nc.dram_tensor("attn_my", [NC_ * TOKS, 256], BF16, kind="Internal").ap()

    ident_h = nc.inline_tensor(np.eye(128, dtype=BFNP), "ident")
    identf_h = nc.inline_tensor(np.eye(128, dtype=np.float32), "identf")
    cm = np.zeros((128, 4 * 512), dtype=np.float32)
    for o4 in range(4):
        for p in range(128):
            j0 = 128 * o4 + p
            if j0 < 512:
                cm[p, 512 * o4 + j0:512 * (o4 + 1)] = 1.0
    cmask_h = nc.inline_tensor(cm.astype(BFNP), "cmask")

    names = [
        "xt0", "xt1", "wqs", "qk", "qke", "v", "ve",
        "S", "exp", "msk", "pv", "ao", "slab0", "slab1", "slab2", "slab3", "slab4", "slab5", "slab6", "slab7", "cc", "o2c", "amy0", "amy1", "amy2", "amy3", "amy4", "amy5", "amy6", "amy7",
        "wos", "aT0", "aT1", "aT2", "aT3", "xr0", "xr1", "o2", "psfree", "var", "rstd", "x2",
        "tp", "tpe", "guw0", "guw1", "g", "u", "sil", "gted", "wd0", "wd1",
        "d3", "d3e", "ob", "outd0", "outd1", "cst", "lnx", "vs", "wco", "wcd", "ccw",
    ]

    VSN = [0]
    import contextlib
    stack = contextlib.ExitStack()
    with stack:
        s = {n: stack.enter_context(nc.semaphore("s_" + n)) for n in names}

        ident_sb = stack.enter_context(nc.sbuf_tensor("ident_sb", [128, 128], BF16))
        identf_sb = stack.enter_context(nc.sbuf_tensor("identf_sb", [128, 128], F32))
        cmask_sb = stack.enter_context(nc.sbuf_tensor("cmask_sb", [128, 2048], BF16))
        stats = stack.enter_context(nc.sbuf_tensor("stats", [128, 8], F32))
        linv = stack.enter_context(nc.sbuf_tensor("linv", [128, 4], F32))
        ysq = stack.enter_context(nc.sbuf_tensor("ysq", [128, 2048], F32))

        with (
            nc.sbuf_tensor("QT", [128, 2 * SEQ], BF16) as QT,
            nc.sbuf_tensor("KT", [128, 2 * SEQ], BF16) as KT,
            nc.sbuf_tensor("Vv", [128, 2 * NKC * VW], BF16) as Vv,
            nc.sbuf_tensor("wq_sb", [128, 16 * 768], BF16) as wq_sb,
            nc.sbuf_tensor("xT0", [128, 16 * WIN], BF16) as xT0,
            nc.sbuf_tensor("xT1", [128, 16 * WIN], BF16) as xT1,
            nc.sbuf_tensor("PT0", [128, 1024], BF16) as PT0,
            nc.sbuf_tensor("PT1", [128, 1024], BF16) as PT1,
            nc.sbuf_tensor("ao_all", [128, 8 * 512], BF16) as ao_all,
            nc.psum_tensor("S0", [128, 1024], F32) as S0,
            nc.psum_tensor("S1", [128, 1024], F32) as S1,
            nc.psum_tensor("att0", [128, 129], F32) as att0,
            nc.psum_tensor("att1", [128, 129], F32) as att1,
            nc.psum_tensor("att2", [128, 129], F32) as att2,
            nc.psum_tensor("att3", [128, 129], F32) as att3,
            nc.Block() as b1,
        ):
            vsn = VSN
            xT = [xT0, xT1]
            PT = [PT0, PT1]
            ao = [ao_all[:, 512 * i:512 * (i + 1)] for i in range(8)]
            S2 = [S0, S1]
            att = [att0, att1, att2, att3]
            qk_ps = [S0[:, 0:512], S0[:, 512:1024]]
            v_ps = [S1[:, 0:256], S1[:, 512:768]]

            @b1.gpsimd
            def _(g):
                g.dma_start(out=wq_sb[:, :], in_=wq).then_inc(s["wqs"], 16)
                for w in range(NWIN):
                    if w >= 2:
                        g.wait_ge(s["v"], 4 * (w - 1))
                    g.dma_start(
                        out=xT[w % 2][:, :].rearrange("p (d c) -> p d c", d=16),
                        in_=xq[w].rearrange("d p c -> p d c"),
                    ).then_inc(s["xt0" if w % 2 == 0 else "xt1"], 16)
                g.dma_start(out=wo_b, in_=wo).then_inc(s["wco"], 16)
                wdcast = [(dh, ho) for dh in range(2) for ho in range(8)]
                for i, k in enumerate(range(NC_)):
                    n_done = 4 * k + 4
                    for par in range(8):
                        cnt = (n_done + 7 - par) // 8
                        if cnt > 0:
                            g.wait_ge(s["slab" + str(par)], 16 * cnt)
                    g.collective_compute(
                        "AllGather", ALU.bypass,
                        replica_groups=[list(range(NC_))],
                        ins=[ag_in[TOKS * k:TOKS * (k + 1), :]], outs=[ag_out[k]],
                    ).then_inc(s["cc"], 1)
                    for dh, ho in wdcast[2 * i:2 * i + 2]:
                        g.dma_start(out=wdT_b[dh, ho], in_=wdT[dh, ho]).then_inc(s["wcd"], 16)


            @b1.sync
            def _(sy):
                sy.dma_start(out=ident_sb[:, :], in_=ident_h.ap()).then_inc(s["cst"], 16)
                sy.dma_start(out=identf_sb[:, :], in_=identf_h.ap()).then_inc(s["cst"], 16)
                sy.dma_start(out=cmask_sb[:, :], in_=cmask_h.ap()).then_inc(s["cst"], 16)
                for sw, (h, qb, npair, pb) in enumerate(SWEEPS):
                    sy.wait_ge(s["ao"], sw + 1)
                    sy.dma_start(
                        out=ag_in.rearrange("(q t p) c -> q p t c", t=4, p=128)[
                            qb, :, :, 128 * h:128 * (h + 1)],
                        in_=ao[sw % 8].rearrange("p (t c) -> p t c", t=4),
                    ).then_inc(s["slab" + str(sw % 8)], 16)

            @b1.vector
            def _(v):
                vv4 = Vv[:, :].rearrange("p (h c w) -> p h c w", h=2, c=NKC, w=VW)
                v.memset(vv4[:, 0, :, 128:129], 1.0)
                v.memset(vv4[:, 1, :, 128:129], 1.0)
                for w in range(NWIN):
                    for ct in range(4):
                        gidx = 4 * w + ct
                        v.wait_ge(s["qk"], gidx + 1)
                        dst = QT if ct < 2 else KT
                        hh = ct % 2
                        v.tensor_copy(
                            dst[:, SEQ * hh + WIN * w: SEQ * hh + WIN * (w + 1)],
                            qk_ps[gidx % 2],
                        ).then_inc(s["qke"], 1)
                    for tt in range(4):
                        vidx = 4 * w + tt
                        cidx = 4 * w + tt
                        v.wait_ge(s["v"], vidx + 1)
                        v.tensor_copy(
                            Vv[:, VW * cidx: VW * cidx + 128],
                            v_ps[vidx % 2][:, 0:128],
                        )
                        v.tensor_copy(
                            Vv[:, NKC * VW + VW * cidx: NKC * VW + VW * cidx + 128],
                            v_ps[vidx % 2][:, 128:256],
                        ).then_inc(s["ve"], 1)
                v.wait_ge(s["cst"], 48)
                for sw, (h, qb, npair, pb) in enumerate(SWEEPS):
                    for which in range(2):
                        pr = npair - 2 + which
                        pg = pb + pr
                        v.wait_ge(s["exp"], pg + 1)
                        for half in range(2):
                            o4 = 2 * which + half
                            ins = v.tensor_tensor(
                                PT[pg % 2][:, 512 * half:512 * (half + 1)],
                                PT[pg % 2][:, 512 * half:512 * (half + 1)],
                                cmask_sb[:, 512 * o4:512 * (o4 + 1)],
                                ALU.mult,
                            )
                            if half == 1:
                                ins.then_inc(s["msk"], 1)
                    v.wait_ge(s["pv"], pb + npair)
                    if sw >= 8:
                        v.wait_ge(s["slab" + str(sw % 8)], 16 * (sw // 8))
                    for t in range(4):
                        ins = v.reciprocal(
                            linv[:, t:t + 1],
                            att[t][:, 128:129],
                        )
                    ins.then_inc(s["vs"], 1)
                    vsn[0] += 1
                    v.wait_ge(s["vs"], vsn[0])
                    for t in range(4):
                        ins = v.tensor_scalar_mul(
                            ao_all[:, 512 * (sw % 8) + 128 * t:512 * (sw % 8) + 128 * (t + 1)],
                            att[t][:, 0:128],
                            linv[:, t:t + 1],
                        )
                        if t == 3:
                            ins.then_inc(s["ao"], 1)

            @b1.scalar
            def _(sc):
                for sw, (h, qb, npair, pb) in enumerate(SWEEPS):
                    for pr in range(npair):
                        pg = pb + pr
                        sc.wait_ge(s["S"], pg + 1)
                        if pg >= 2:
                            sc.wait_ge(s["pv"], pg - 1)
                        sc.activation(
                            out=PT[pg % 2][:, :], in_=S2[pg % 2][:, :],
                            func=AF.Exp, bias=-CEXP, scale=SSCL,
                        ).then_inc(s["exp"], 1)

            @b1.tensor
            def _(t):
                t.wait_ge(s["wqs"], 16)
                for w in range(NWIN):
                    t.wait_ge(s["xt0" if w % 2 == 0 else "xt1"], 16 * (w // 2 + 1))
                    for ct in range(4):
                        gidx = 4 * w + ct
                        if gidx >= 2:
                            t.wait_ge(s["qke"], gidx - 1)
                        for d in range(16):
                            ins = t.matmul(
                                qk_ps[gidx % 2],
                                wq_sb[:, 768 * d + 128 * ct:768 * d + 128 * (ct + 1)],
                                xT[w % 2][:, WIN * d:WIN * (d + 1)],
                                start=(d == 0), stop=(d == 15),
                            )
                        ins.then_inc(s["qk"], 1)
                    for tt in range(4):
                        vidx = 4 * w + tt
                        if vidx >= 2:
                            t.wait_ge(s["ve"], vidx - 1)
                        for d in range(16):
                            ins = t.matmul(
                                v_ps[vidx % 2],
                                xT[w % 2][:, WIN * d + 128 * tt:WIN * d + 128 * (tt + 1)],
                                wq_sb[:, 768 * d + 512:768 * (d + 1)],
                                start=(d == 0), stop=(d == 15),
                            )
                        ins.then_inc(s["v"], 1)
                t.wait_ge(s["qke"], 64)
                t.wait_ge(s["ve"], 64)

                def emit_S(sw, pr):
                    h, qb, npair, pb = SWEEPS[sw]
                    pg = pb + pr
                    if pg >= 2:
                        t.wait_ge(s["exp"], pg - 1)
                    for half in range(2):
                        c = 2 * pr + half
                        ins = t.matmul(
                            S2[pg % 2][:, 512 * half:512 * (half + 1)],
                            KT[:, SEQ * h + 128 * c:SEQ * h + 128 * (c + 1)],
                            QT[:, SEQ * h + QB * qb:SEQ * h + QB * (qb + 1)],
                            start=True, stop=True,
                        )
                    ins.then_inc(s["S"], 1)

                def emit_PV(sw, pr):
                    h, qb, npair, pb = SWEEPS[sw]
                    pg = pb + pr
                    nk = 2 * npair
                    diag = pr >= npair - 2
                    if diag:
                        mid = 2 * sw + (pr - (npair - 2))
                        t.wait_ge(s["msk"], mid + 1)
                    else:
                        t.wait_ge(s["exp"], pg + 1)
                    if pr == 0:
                        t.wait_ge(s["ao"], sw)
                    for half in range(2):
                        c = 2 * pr + half
                        for q4 in range(4):
                            ins = t.matmul(
                                att[q4][:, 0:129],
                                PT[pg % 2][:, 512 * half + 128 * q4:512 * half + 128 * (q4 + 1)],
                                Vv[:, NKC * VW * h + VW * c:NKC * VW * h + VW * c + 129],
                                start=(c == 0),
                                stop=(c == nk - 1),
                            )
                    ins.then_inc(s["pv"], 1)

                for sw, (h, qb, npair, pb) in enumerate(SWEEPS):
                    emit_S(sw, 0)
                    if npair > 1:
                        emit_S(sw, 1)
                    for pr in range(npair):
                        if pr + 2 < npair:
                            emit_S(sw, pr + 2)
                        emit_PV(sw, pr)

        with (
            nc.sbuf_tensor("x2", [128, 8 * 2048], F32) as x2,
            nc.sbuf_tensor("ob0", [128, 2048], F32) as ob0,
            nc.sbuf_tensor("ob1", [128, 2048], F32) as ob1,
        ):
            out_sb = [ob0, ob1]

            vsn = VSN

            def vssync(v, ins):
                ins.then_inc(s["vs"], 1)
                vsn[0] += 1
                v.wait_ge(s["vs"], vsn[0])

            def ln_stats_emit(v, ysrc):
                v.reduce_sum(stats[:, 0:1], ysrc, axis=AX.X)
                vssync(v, v.scalar_tensor_tensor(
                    ysq[:, :], ysrc, 1.0, ysrc, ALU.mult, ALU.mult,
                    accum_out=stats[:, 1:2],
                ))
                v.tensor_scalar_mul(stats[:, 2:3], stats[:, 0:1], 1.0 / 2048.0)
                vssync(v, v.tensor_scalar_mul(stats[:, 3:4], stats[:, 1:2], 1.0 / 2048.0))
                vssync(v, v.tensor_tensor(stats[:, 4:5], stats[:, 2:3], stats[:, 2:3],
                                          ALU.mult))
                v.tensor_tensor(stats[:, 5:6], stats[:, 3:4], stats[:, 4:5],
                                ALU.subtract).then_inc(s["var"], 1)

            lnx_n = [0]

            def rstd_emit(sc):
                lnx_n[0] += 1
                sc.activation(out=stats[:, 6:7], in_=stats[:, 5:6],
                              func=AF.Ln, bias=f32(1e-5)).then_inc(s["lnx"], 1)
                sc.wait_ge(s["lnx"], lnx_n[0])
                sc.activation(out=stats[:, 7:8], in_=stats[:, 6:7],
                              func=AF.Exp, scale=f32(-0.5)).then_inc(s["rstd"], 1)

            with (
                nc.sbuf_tensor("attnT", [128, 16 * 512], BF16) as attnT,
                nc.sbuf_tensor("wout_sb", [128, 16 * 2048], BF16) as wout_sb,
                nc.sbuf_tensor("xres0", [128, 2048], F32) as xres0,
                nc.sbuf_tensor("xres1", [128, 2048], F32) as xres1,
                nc.sbuf_tensor("y0", [128, 2048], F32) as y0,
                nc.sbuf_tensor("y1", [128, 2048], F32) as y1,
                nc.psum_tensor("o2a", [128, 2048], F32) as o2a,
                nc.psum_tensor("o2b", [128, 2048], F32) as o2b,
                nc.Block() as bA,
            ):
                xres_sb = [xres0, xres1]
                y_sb = [y0, y1]
                o2 = [o2a, o2b]

                @bA.gpsimd
                def _(g):
                    g.wait_ge(s["wco"], 16)
                    g.dma_start(out=wout_sb[:, :], in_=wo_b).then_inc(s["wos"], 16)
                    for gt in range(2):
                        g.dma_start(
                            out=xres_sb[gt % 2][:, :],
                            in_=xres[128 * gt:128 * (gt + 1), :],
                        ).then_inc(s["xr0" if gt % 2 == 0 else "xr1"], 16)
                    pid = nc.partition_id(engines=[mybir.EngineType.Pool])
                    g.wait_ge(s["cc"], pid + 1)
                    flat = ag_out.rearrange("a b c -> (a b) c")
                    for r in range(NC_):
                        g.dma_start(
                            out=attn_my[TOKS * r:TOKS * (r + 1), :],
                            in_=flat[bass.ds(pid * (NC_ * TOKS) + TOKS * r, TOKS), :],
                        ).then_inc(s["amy" + str(r)], 16)
                    for gt in range(2, 8):
                        g.wait_ge(s["psfree"], gt - 1)
                        g.dma_start(
                            out=xres_sb[gt % 2][:, :],
                            in_=xres[128 * gt:128 * (gt + 1), :],
                        ).then_inc(s["xr0" if gt % 2 == 0 else "xr1"], 16)

                @bA.scalar
                def _(sc):
                    for ch in range(16):
                        sc.wait_ge(s["amy" + str(ch // 2)], 16)
                        sc.dma_start_transpose(
                            attnT[:, 512 * ch:512 * (ch + 1)],
                            attn_my[TOKS * (ch // 2):TOKS * (ch // 2) + 512,
                                    128 * (ch % 2):128 * (ch % 2) + 128],
                        ).then_inc(s["aT" + str(ch // 4)], 16)
                    for gt in range(3):
                        sc.wait_ge(s["var"], gt + 1)
                        rstd_emit(sc)
                    for ch in range(16):
                        if ch < 15:
                            sc.wait_ge(s["o2c"], ch + 1)
                        else:
                            sc.wait_ge(s["o2"], 4)
                        sc.dma_start_transpose(
                            attnT[:, 512 * ch:512 * (ch + 1)],
                            attn_my[TOKS * (ch // 2) + QB:
                                    TOKS * (ch // 2) + QB + 512,
                                    128 * (ch % 2):128 * (ch % 2) + 128],
                        ).then_inc(s["aT" + str(ch // 4)], 16)
                    for gt in range(3, 8):
                        sc.wait_ge(s["var"], gt + 1)
                        rstd_emit(sc)

                @bA.vector
                def _(v):
                    for gt in range(8):
                        v.wait_ge(s["o2"], gt + 1)
                        v.wait_ge(s["xr0" if gt % 2 == 0 else "xr1"], 16 * (gt // 2 + 1))
                        v.tensor_add(y_sb[gt % 2][:, :], o2[gt % 2][:, :],
                                     xres_sb[gt % 2][:, :]).then_inc(s["psfree"], 1)
                        v.wait_ge(s["psfree"], gt + 1)
                        ln_stats_emit(v, y_sb[gt % 2][:, :])
                        v.wait_ge(s["rstd"], gt + 1)
                        v.tensor_scalar(
                            x2[:, 2048 * gt:2048 * (gt + 1)], y_sb[gt % 2][:, :],
                            stats[:, 2:3], stats[:, 7:8], ALU.subtract, ALU.mult,
                        ).then_inc(s["x2"], 1)

                @bA.tensor
                def _(t):
                    t.wait_ge(s["wos"], 16)
                    for gt in range(8):
                        tc, tt = gt // 4, gt % 4
                        if gt >= 2:
                            t.wait_ge(s["psfree"], gt - 1)
                        for ch in range(16):
                            if (gt == 0 or gt == 4) and ch % 4 == 0:
                                t.wait_ge(s["aT" + str(ch // 4)], 64 * (tc + 1))
                            for j in range(4):
                                ins = t.matmul(
                                    o2[gt % 2][:, 512 * j:512 * (j + 1)],
                                    attnT[:, 512 * ch + 128 * tt:512 * ch + 128 * (tt + 1)],
                                    wout_sb[:, 2048 * ch + 512 * j:2048 * ch + 512 * (j + 1)],
                                    start=(ch == 0), stop=(ch == 15),
                                )
                                if gt == 3 and j == 3 and ch < 15:
                                    ins.then_inc(s["o2c"], 1)
                        ins.then_inc(s["o2"], 1)

            for tc in range(2):
                with (
                    nc.sbuf_tensor(f"gatedT_{tc}", [128, 64 * 512], BF16) as gatedT,
                    nc.sbuf_tensor(f"gsil0_{tc}", [128, 512], BF16) as gsil0,
                    nc.sbuf_tensor(f"gsil1_{tc}", [128, 512], BF16) as gsil1,
                ):
                    gsil = [gsil0, gsil1]
                    with (
                        nc.sbuf_tensor(f"x2T_{tc}", [128, 16 * 512], BF16) as x2T,
                        nc.sbuf_tensor(f"guw0_{tc}", [128, 2 * 16 * 256], BF16) as guw0,
                        nc.sbuf_tensor(f"guw1_{tc}", [128, 2 * 16 * 256], BF16) as guw1,
                        nc.psum_tensor(f"tp0_{tc}", [128, 128], F32) as tp0,
                        nc.psum_tensor(f"tp1_{tc}", [128, 128], F32) as tp1,
                        nc.psum_tensor(f"tp2_{tc}", [128, 128], F32) as tp2,
                        nc.psum_tensor(f"tp3_{tc}", [128, 128], F32) as tp3,
                        nc.psum_tensor(f"gA0_{tc}", [128, 512], F32) as gA0,
                        nc.psum_tensor(f"gA1_{tc}", [128, 512], F32) as gA1,
                        nc.psum_tensor(f"uA0_{tc}", [128, 512], F32) as uA0,
                        nc.psum_tensor(f"uA1_{tc}", [128, 512], F32) as uA1,
                        nc.Block() as bB,
                    ):
                        guw = [guw0, guw1]
                        gA = [gA0, gA1]
                        uA = [uA0, uA1]
                        tp_ps = [tp0, tp1, tp2, tp3]

                        @bB.gpsimd
                        def _(g, tc=tc):
                            if tc == 0:
                                g.wait_ge(s["o2"], 8)
                                g.wait_ge(s["x2"], 8)
                            else:
                                g.wait_ge(s["d3e"], 64)
                            for hb in range(32):
                                ghb = 32 * tc + hb
                                if ghb >= 2:
                                    g.wait_ge(s["gted"], 2 * (ghb - 1))
                                gp = "guw0" if ghb % 2 == 0 else "guw1"
                                g.dma_start(
                                    out=guw[ghb % 2][:, 0:4096].rearrange("p (d c) -> p d c", d=16),
                                    in_=wgT[hb].rearrange("d p c -> p d c"),
                                ).then_inc(s[gp], 16)
                                g.dma_start(
                                    out=guw[ghb % 2][:, 4096:8192].rearrange("p (d c) -> p d c", d=16),
                                    in_=wuT[hb].rearrange("d p c -> p d c"),
                                ).then_inc(s[gp], 16)

                        @bB.scalar
                        def _(sc, tc=tc):
                            for k in range(64):
                                tj = 64 * tc + k
                                sc.wait_ge(s["tp"], tj + 1)
                                tt, j = k // 16, k % 16
                                sc.copy(
                                    x2T[:, 512 * j + 128 * tt:512 * j + 128 * tt + 128],
                                    tp_ps[tj % 4][:, :],
                                ).then_inc(s["tpe"], 1)
                            for gct0 in range(64):
                                gct = 64 * tc + gct0
                                sc.wait_ge(s["g"], gct + 1)
                                if gct >= 2:
                                    sc.wait_ge(s["gted"], gct - 1)
                                sc.activation(out=gsil[gct % 2][:, :], in_=gA[gct % 2][:, :],
                                              func=AF.Silu).then_inc(s["sil"], 1)
                            if tc == 1:
                                for tt in range(4):
                                    sc.wait_ge(s["var"], 8 + tt + 1)
                                    rstd_emit(sc)

                        @bB.vector
                        def _(v, tc=tc):
                            for gct0 in range(64):
                                gct = 64 * tc + gct0
                                H = gct0
                                v.wait_ge(s["sil"], gct + 1)
                                v.wait_ge(s["u"], gct + 1)
                                v.tensor_tensor(
                                    gatedT[:, 512 * H:512 * (H + 1)],
                                    gsil[gct % 2][:, :], uA[gct % 2][:, :], ALU.mult,
                                ).then_inc(s["gted"], 1)
                            if tc == 1:
                                v.wait_ge(s["d3e"], 64)
                                for tt in range(4):
                                    gt = tt
                                    ln_stats_emit(v, x2[:, 2048 * gt:2048 * (gt + 1)])
                                    v.wait_ge(s["rstd"], 8 + tt + 1)
                                    if gt >= 2:
                                        v.wait_ge(s["outd0" if gt % 2 == 0 else "outd1"],
                                                  16 * (gt // 2))
                                    v.tensor_scalar(
                                        out_sb[gt % 2][:, :], x2[:, 2048 * gt:2048 * (gt + 1)],
                                        stats[:, 2:3], stats[:, 7:8], ALU.subtract, ALU.mult,
                                    ).then_inc(s["ob"], 1)

                        @bB.sync
                        def _(sy, tc=tc):
                            if tc == 1:
                                for tt in range(4):
                                    gt = tt
                                    sy.wait_ge(s["ob"], gt + 1)
                                    sy.dma_start(
                                        out=out[128 * gt:128 * (gt + 1), :],
                                        in_=out_sb[gt % 2][:, :],
                                    ).then_inc(s["outd0" if gt % 2 == 0 else "outd1"], 16)

                        @bB.tensor
                        def _(t, tc=tc):
                            for k in range(64):
                                tj = 64 * tc + k
                                tt, j = k // 16, k % 16
                                gt = 4 * tc + tt
                                t.wait_ge(s["x2"], gt + 1)
                                if tj >= 4:
                                    t.wait_ge(s["tpe"], tj - 3)
                                t.transpose(tp_ps[tj % 4][:, :],
                                            x2[:, 2048 * gt + 128 * j:2048 * gt + 128 * (j + 1)],
                                            identf_sb[:, :]).then_inc(s["tp"], 1)
                            t.wait_ge(s["tpe"], 64 * (tc + 1))
                            for hb in range(32):
                                ghb = 32 * tc + hb
                                t.wait_ge(s["guw0" if ghb % 2 == 0 else "guw1"],
                                          32 * (ghb // 2 + 1))
                                for ht in range(2):
                                    gct = 64 * tc + 2 * hb + ht
                                    if gct >= 2:
                                        t.wait_ge(s["sil"], gct - 1)
                                    for d in range(16):
                                        ins = t.matmul(
                                            gA[gct % 2][:, :],
                                            guw[ghb % 2][:, 256 * d + 128 * ht:256 * d + 128 * (ht + 1)],
                                            x2T[:, 512 * d:512 * (d + 1)],
                                            start=(d == 0), stop=(d == 15),
                                        )
                                    ins.then_inc(s["g"], 1)
                                    if gct >= 2:
                                        t.wait_ge(s["gted"], gct - 1)
                                    for d in range(16):
                                        ins = t.matmul(
                                            uA[gct % 2][:, :],
                                            guw[ghb % 2][:, 4096 + 256 * d + 128 * ht:
                                                          4096 + 256 * d + 128 * (ht + 1)],
                                            x2T[:, 512 * d:512 * (d + 1)],
                                            start=(d == 0), stop=(d == 15),
                                        )
                                    ins.then_inc(s["u"], 1)

                    with (
                        nc.sbuf_tensor(f"wdT0_{tc}", [128, 8 * 1024], BF16) as wdT0,
                        nc.sbuf_tensor(f"wdT1_{tc}", [128, 8 * 1024], BF16) as wdT1,
                        nc.sbuf_tensor(f"p3_{tc}", [128, 4 * 1024], F32) as p3,
                        nc.psum_tensor(f"d3_{tc}", [128, 2048], F32) as d3,
                        nc.Block() as bC,
                    ):
                        wdT_sb = [wdT0, wdT1]

                        @bC.gpsimd
                        def _(g, tc=tc):
                            g.wait_ge(s["wcd"], 16 * 16)
                            for dh in range(2):
                                for ho in range(8):
                                    gb = 16 * tc + 8 * dh + ho
                                    if gb >= 2:
                                        g.wait_ge(s["d3"], 4 * (gb - 1))
                                    g.dma_start(
                                        out=wdT_sb[gb % 2][:, :].rearrange("p (k c) -> p k c", k=8),
                                        in_=wdT_b[dh, ho].rearrange("k p c -> p k c"),
                                    ).then_inc(s["wd0" if gb % 2 == 0 else "wd1"], 16)

                        @bC.scalar
                        def _(sc, tc=tc):
                            if tc == 1:
                                for tt in range(4):
                                    sc.wait_ge(s["var"], 12 + tt + 1)
                                    rstd_emit(sc)

                        @bC.vector
                        def _(v, tc=tc):
                            for dh in range(2):
                                for ho in range(8):
                                    for tt in range(4):
                                        gt = 4 * tc + tt
                                        dct = 64 * tc + 32 * dh + 4 * ho + tt
                                        v.wait_ge(s["d3"], dct + 1)
                                        dps = d3[:, 1024 * (tt % 2):1024 * (tt % 2 + 1)]
                                        x2sl = x2[:, 2048 * gt + 1024 * dh:
                                                  2048 * gt + 1024 * (dh + 1)]
                                        p3sl = p3[:, 1024 * tt:1024 * (tt + 1)]
                                        if ho == 0:
                                            v.tensor_add(p3sl, dps, x2sl).then_inc(s["d3e"], 1)
                                        elif ho < 7:
                                            v.tensor_add(p3sl, dps, p3sl).then_inc(s["d3e"], 1)
                                        else:
                                            v.tensor_add(x2sl, dps, p3sl).then_inc(s["d3e"], 1)
                            if tc == 1:
                                v.wait_ge(s["d3e"], 128)
                                for tt in range(4):
                                    gt = 4 + tt
                                    ln_stats_emit(v, x2[:, 2048 * gt:2048 * (gt + 1)])
                                    v.wait_ge(s["rstd"], 12 + tt + 1)
                                    v.wait_ge(s["outd0" if gt % 2 == 0 else "outd1"],
                                              16 * (gt // 2))
                                    v.tensor_scalar(
                                        out_sb[gt % 2][:, :], x2[:, 2048 * gt:2048 * (gt + 1)],
                                        stats[:, 2:3], stats[:, 7:8], ALU.subtract, ALU.mult,
                                    ).then_inc(s["ob"], 1)

                        @bC.sync
                        def _(sy, tc=tc):
                            if tc == 1:
                                for tt in range(4):
                                    gt = 4 + tt
                                    sy.wait_ge(s["ob"], gt + 1)
                                    sy.dma_start(
                                        out=out[128 * gt:128 * (gt + 1), :],
                                        in_=out_sb[gt % 2][:, :],
                                    ).then_inc(s["outd0" if gt % 2 == 0 else "outd1"], 16)
                                sy.wait_ge(s["outd0"], 16 * 4)
                                sy.wait_ge(s["outd1"], 16 * 4)

                        @bC.tensor
                        def _(t, tc=tc):
                            for dh in range(2):
                                for ho in range(8):
                                    gb = 16 * tc + 8 * dh + ho
                                    t.wait_ge(s["wd0" if gb % 2 == 0 else "wd1"],
                                              16 * (gb // 2 + 1))
                                    if dh == 0 and ho == 0:
                                        t.wait_ge(s["gted"], 64 * (tc + 1))
                                    for tt in range(4):
                                        dct = 64 * tc + 32 * dh + 4 * ho + tt
                                        if dct >= 2:
                                            t.wait_ge(s["d3e"], dct - 1)
                                        for hc in range(8):
                                            H = 8 * ho + hc
                                            for j in range(2):
                                                ins = t.matmul(
                                                    d3[:, 1024 * (tt % 2) + 512 * j:
                                                       1024 * (tt % 2) + 512 * (j + 1)],
                                                    gatedT[:, 512 * H + 128 * tt:
                                                           512 * H + 128 * (tt + 1)],
                                                    wdT_sb[gb % 2][:, 1024 * hc + 512 * j:
                                                                   1024 * hc + 512 * (j + 1)],
                                                    start=(hc == 0), stop=(hc == 7),
                                                )
                                        ins.then_inc(s["d3"], 1)

    return nc


_NC_CACHE = None


def _get_nc():
    global _NC_CACHE
    if _NC_CACHE is None:
        _NC_CACHE = build_nc()
    return _NC_CACHE


def kernel(L0_attn_x, L0_attn_w_qkv, L0_attn_w_out, L0_ffn_w_gate,
           L0_ffn_w_up, L0_ffn_w_down):
    x = np.ascontiguousarray(np.asarray(L0_attn_x, dtype=np.float32).reshape(SEQ, DM))
    w_qkv = np.asarray(L0_attn_w_qkv, dtype=np.float32)
    w_out = np.asarray(L0_attn_w_out, dtype=np.float32)
    w_g = np.asarray(L0_ffn_w_gate, dtype=np.float32)
    w_u = np.asarray(L0_ffn_w_up, dtype=np.float32)
    w_d = np.asarray(L0_ffn_w_down, dtype=np.float32)

    xq = np.ascontiguousarray(x.reshape(NWIN, WIN, 16, 128).transpose(0, 2, 3, 1))
    wo = np.ascontiguousarray(
        w_out.reshape(16, 128, 2048).transpose(1, 0, 2).reshape(128, 16 * 2048))
    wgT = np.ascontiguousarray(w_g.reshape(32, 256, 16, 128).transpose(0, 2, 3, 1))
    wuT = np.ascontiguousarray(w_u.reshape(32, 256, 16, 128).transpose(0, 2, 3, 1))
    wdT = np.ascontiguousarray(
        w_d.reshape(2, 1024, 8, 8, 128).transpose(0, 2, 3, 4, 1))

    in_maps = []
    for c in range(NC_):
        h0, h1 = 2 * c, 2 * c + 1
        blocks = [w_qkv[:, h0 * 128:(h0 + 1) * 128],
                  w_qkv[:, h1 * 128:(h1 + 1) * 128],
                  w_qkv[:, 2048 + h0 * 128:2048 + (h0 + 1) * 128],
                  w_qkv[:, 2048 + h1 * 128:2048 + (h1 + 1) * 128],
                  w_qkv[:, 4096 + h0 * 128:4096 + (h0 + 1) * 128],
                  w_qkv[:, 4096 + h1 * 128:4096 + (h1 + 1) * 128]]
        wq_c = np.stack(blocks, axis=1)
        wq_c = np.ascontiguousarray(
            wq_c.reshape(16, 128, 6, 128).transpose(1, 0, 2, 3).reshape(128, 16 * 768))
        in_maps.append({
            "xq": xq, "wq": wq_c, "wo": wo, "wgT": wgT, "wuT": wuT,
            "wdT": wdT, "xres": np.ascontiguousarray(x[TOKS * c:TOKS * (c + 1)]),
        })

    nc = _get_nc()
    res = run_bass_kernel_spmd(nc, in_maps, core_ids=list(range(NC_)))
    out = np.concatenate([r["out"] for r in res.results], axis=0)
    return out.reshape(1, SEQ, DM)


# version 24
# speedup vs baseline: 1.0058x; 1.0058x over previous
import numpy as np
import concourse.bass as bass
import concourse.mybir as mybir
from concourse.bass_utils import run_bass_kernel_spmd

import ml_dtypes
BFNP = ml_dtypes.bfloat16

F32 = mybir.dt.float32
BF16 = mybir.dt.bfloat16
AF = mybir.ActivationFunctionType
ALU = mybir.AluOpType
AX = mybir.AxisListType
f32 = np.float32

SEQ = 8192
DM = 2048
NC_ = 8
TOKS = SEQ // NC_
NWIN = 16
WIN = 512
NQB = 16
QB = 512
CEXP = f32(5.0)
SSCL = f32(1.0 / np.sqrt(128.0))
VW = 132
NKC = 64


def _sweeps():
    out = []
    pb = 0
    for qb in range(NQB):
        for h in range(2):
            npair = 2 * (qb + 1)
            out.append((h, qb, npair, pb))
            pb += npair
    return out


SWEEPS = _sweeps()


def build_nc():
    nc = bass.Bass("TRN2", target_bir_lowering=False, debug=False, num_devices=NC_,
                   detect_race_conditions=True)

    xq = nc.dram_tensor("xq", [NWIN, 16, 128, WIN], F32, kind="ExternalInput").ap()
    wq = nc.dram_tensor("wq", [128, 16 * 768], F32, kind="ExternalInput").ap()
    wo = nc.dram_tensor("wo", [128, 16 * 2048], F32, kind="ExternalInput").ap()
    wgT = nc.dram_tensor("wgT", [32, 16, 128, 256], F32, kind="ExternalInput").ap()
    wuT = nc.dram_tensor("wuT", [32, 16, 128, 256], F32, kind="ExternalInput").ap()
    wdT = nc.dram_tensor("wdT", [2, 8, 8, 128, 1024], F32, kind="ExternalInput").ap()
    xres = nc.dram_tensor("xres", [TOKS, DM], F32, kind="ExternalInput").ap()
    out = nc.dram_tensor("out", [TOKS, DM], F32, kind="ExternalOutput").ap()

    ag_in = nc.dram_tensor("ag_in", [SEQ, 256], BF16, kind="Internal").ap()
    ag_out = nc.dram_tensor("ag_out", [NC_, NC_ * TOKS, 256], BF16, kind="Internal").ap()
    wo_b = nc.dram_tensor("wo_b", [128, 16 * 2048], BF16, kind="Internal").ap()
    wdT_b = nc.dram_tensor("wdT_b", [2, 8, 8, 128, 1024], BF16, kind="Internal").ap()
    attn_my = nc.dram_tensor("attn_my", [NC_ * TOKS, 256], BF16, kind="Internal").ap()

    ident_h = nc.inline_tensor(np.eye(128, dtype=BFNP), "ident")
    identf_h = nc.inline_tensor(np.eye(128, dtype=np.float32), "identf")
    cm = np.zeros((128, 4 * 512), dtype=np.float32)
    for o4 in range(4):
        for p in range(128):
            j0 = 128 * o4 + p
            if j0 < 512:
                cm[p, 512 * o4 + j0:512 * (o4 + 1)] = 1.0
    cmask_h = nc.inline_tensor(cm.astype(BFNP), "cmask")

    names = [
        "xt0", "xt1", "wqs", "qk", "qke", "v", "ve",
        "S", "exp", "msk", "pv", "ao", "slab0", "slab1", "slab2", "slab3", "cc", "o2c", "amy0", "amy1", "amy2", "amy3", "amy4", "amy5", "amy6", "amy7",
        "wos", "aT0", "aT1", "aT2", "aT3", "xr0", "xr1", "o2", "psfree", "var", "rstd", "x2",
        "tp", "tpe", "guw0", "guw1", "g", "u", "sil", "gted", "wd0", "wd1",
        "d3", "d3e", "ob", "outd0", "outd1", "cst", "lnx", "vs", "wco", "wcd", "ccw",
    ]

    VSN = [0]
    import contextlib
    stack = contextlib.ExitStack()
    with stack:
        s = {n: stack.enter_context(nc.semaphore("s_" + n)) for n in names}

        ident_sb = stack.enter_context(nc.sbuf_tensor("ident_sb", [128, 128], BF16))
        identf_sb = stack.enter_context(nc.sbuf_tensor("identf_sb", [128, 128], F32))
        cmask_sb = stack.enter_context(nc.sbuf_tensor("cmask_sb", [128, 2048], BF16))
        stats = stack.enter_context(nc.sbuf_tensor("stats", [128, 8], F32))
        linv = stack.enter_context(nc.sbuf_tensor("linv", [128, 4], F32))
        ysq = stack.enter_context(nc.sbuf_tensor("ysq", [128, 2048], F32))

        with (
            nc.sbuf_tensor("QT", [128, 2 * SEQ], BF16) as QT,
            nc.sbuf_tensor("KT", [128, 2 * SEQ], BF16) as KT,
            nc.sbuf_tensor("Vv", [128, 2 * NKC * VW], BF16) as Vv,
            nc.sbuf_tensor("wq_sb", [128, 16 * 768], BF16) as wq_sb,
            nc.sbuf_tensor("xT0", [128, 16 * WIN], BF16) as xT0,
            nc.sbuf_tensor("xT1", [128, 16 * WIN], BF16) as xT1,
            nc.sbuf_tensor("PT0", [128, 1024], BF16) as PT0,
            nc.sbuf_tensor("PT1", [128, 1024], BF16) as PT1,
            nc.sbuf_tensor("ao0", [128, 512], BF16) as ao0,
            nc.sbuf_tensor("ao1", [128, 512], BF16) as ao1,
            nc.sbuf_tensor("ao2", [128, 512], BF16) as ao2,
            nc.sbuf_tensor("ao3", [128, 512], BF16) as ao3,
            nc.psum_tensor("S0", [128, 1024], F32) as S0,
            nc.psum_tensor("S1", [128, 1024], F32) as S1,
            nc.psum_tensor("att0", [128, 129], F32) as att0,
            nc.psum_tensor("att1", [128, 129], F32) as att1,
            nc.psum_tensor("att2", [128, 129], F32) as att2,
            nc.psum_tensor("att3", [128, 129], F32) as att3,
            nc.Block() as b1,
        ):
            vsn = VSN
            xT = [xT0, xT1]
            PT = [PT0, PT1]
            ao = [ao0, ao1, ao2, ao3]
            S2 = [S0, S1]
            att = [att0, att1, att2, att3]
            qk_ps = [S0[:, 0:512], S0[:, 512:1024]]
            v_ps = [S1[:, 0:256], S1[:, 512:768]]

            @b1.gpsimd
            def _(g):
                g.dma_start(out=wq_sb[:, :], in_=wq).then_inc(s["wqs"], 16)
                for w in range(NWIN):
                    if w >= 2:
                        g.wait_ge(s["v"], 4 * (w - 1))
                    g.dma_start(
                        out=xT[w % 2][:, :].rearrange("p (d c) -> p d c", d=16),
                        in_=xq[w].rearrange("d p c -> p d c"),
                    ).then_inc(s["xt0" if w % 2 == 0 else "xt1"], 16)
                g.dma_start(out=wo_b, in_=wo).then_inc(s["wco"], 16)
                wdcast = [(dh, ho) for dh in range(2) for ho in range(8)]
                for i, k in enumerate(range(NC_)):
                    n_done = 4 * k + 4
                    for par in range(4):
                        g.wait_ge(s["slab" + str(par)], 16 * (n_done // 4))
                    g.collective_compute(
                        "AllGather", ALU.bypass,
                        replica_groups=[list(range(NC_))],
                        ins=[ag_in[TOKS * k:TOKS * (k + 1), :]], outs=[ag_out[k]],
                    ).then_inc(s["cc"], 1)
                    for dh, ho in wdcast[2 * i:2 * i + 2]:
                        g.dma_start(out=wdT_b[dh, ho], in_=wdT[dh, ho]).then_inc(s["wcd"], 16)


            @b1.sync
            def _(sy):
                sy.dma_start(out=ident_sb[:, :], in_=ident_h.ap()).then_inc(s["cst"], 16)
                sy.dma_start(out=identf_sb[:, :], in_=identf_h.ap()).then_inc(s["cst"], 16)
                sy.dma_start(out=cmask_sb[:, :], in_=cmask_h.ap()).then_inc(s["cst"], 16)
                for sw, (h, qb, npair, pb) in enumerate(SWEEPS):
                    sy.wait_ge(s["ao"], sw + 1)
                    sy.dma_start(
                        out=ag_in.rearrange("(q t p) c -> q p t c", t=4, p=128)[
                            qb, :, :, 128 * h:128 * (h + 1)],
                        in_=ao[sw % 4][:, :].rearrange("p (t c) -> p t c", t=4),
                    ).then_inc(s["slab" + str(sw % 4)], 16)

            @b1.vector
            def _(v):
                vv4 = Vv[:, :].rearrange("p (h c w) -> p h c w", h=2, c=NKC, w=VW)
                v.memset(vv4[:, 0, :, 128:129], 1.0)
                v.memset(vv4[:, 1, :, 128:129], 1.0)
                for w in range(NWIN):
                    for ct in range(4):
                        gidx = 4 * w + ct
                        v.wait_ge(s["qk"], gidx + 1)
                        dst = QT if ct < 2 else KT
                        hh = ct % 2
                        v.tensor_copy(
                            dst[:, SEQ * hh + WIN * w: SEQ * hh + WIN * (w + 1)],
                            qk_ps[gidx % 2],
                        ).then_inc(s["qke"], 1)
                    for tt in range(4):
                        vidx = 4 * w + tt
                        cidx = 4 * w + tt
                        v.wait_ge(s["v"], vidx + 1)
                        v.tensor_copy(
                            Vv[:, VW * cidx: VW * cidx + 128],
                            v_ps[vidx % 2][:, 0:128],
                        )
                        v.tensor_copy(
                            Vv[:, NKC * VW + VW * cidx: NKC * VW + VW * cidx + 128],
                            v_ps[vidx % 2][:, 128:256],
                        ).then_inc(s["ve"], 1)
                v.wait_ge(s["cst"], 48)
                for sw, (h, qb, npair, pb) in enumerate(SWEEPS):
                    for which in range(2):
                        pr = npair - 2 + which
                        pg = pb + pr
                        v.wait_ge(s["exp"], pg + 1)
                        for half in range(2):
                            o4 = 2 * which + half
                            ins = v.tensor_tensor(
                                PT[pg % 2][:, 512 * half:512 * (half + 1)],
                                PT[pg % 2][:, 512 * half:512 * (half + 1)],
                                cmask_sb[:, 512 * o4:512 * (o4 + 1)],
                                ALU.mult,
                            )
                            if half == 1:
                                ins.then_inc(s["msk"], 1)
                    v.wait_ge(s["pv"], pb + npair)
                    if sw >= 4:
                        v.wait_ge(s["slab" + str(sw % 4)], 16 * (sw // 4))
                    for t in range(4):
                        ins = v.reciprocal(
                            linv[:, t:t + 1],
                            att[t][:, 128:129],
                        )
                    ins.then_inc(s["vs"], 1)
                    vsn[0] += 1
                    v.wait_ge(s["vs"], vsn[0])
                    for t in range(4):
                        ins = v.tensor_scalar_mul(
                            ao[sw % 4][:, 128 * t:128 * (t + 1)],
                            att[t][:, 0:128],
                            linv[:, t:t + 1],
                        )
                        if t == 3:
                            ins.then_inc(s["ao"], 1)

            @b1.scalar
            def _(sc):
                for sw, (h, qb, npair, pb) in enumerate(SWEEPS):
                    for pr in range(npair):
                        pg = pb + pr
                        sc.wait_ge(s["S"], pg + 1)
                        if pg >= 2:
                            sc.wait_ge(s["pv"], pg - 1)
                        sc.activation(
                            out=PT[pg % 2][:, :], in_=S2[pg % 2][:, :],
                            func=AF.Exp, bias=-CEXP, scale=SSCL,
                        ).then_inc(s["exp"], 1)

            @b1.tensor
            def _(t):
                t.wait_ge(s["wqs"], 16)
                for w in range(NWIN):
                    t.wait_ge(s["xt0" if w % 2 == 0 else "xt1"], 16 * (w // 2 + 1))
                    for ct in range(4):
                        gidx = 4 * w + ct
                        if gidx >= 2:
                            t.wait_ge(s["qke"], gidx - 1)
                        for d in range(16):
                            ins = t.matmul(
                                qk_ps[gidx % 2],
                                wq_sb[:, 768 * d + 128 * ct:768 * d + 128 * (ct + 1)],
                                xT[w % 2][:, WIN * d:WIN * (d + 1)],
                                start=(d == 0), stop=(d == 15),
                            )
                        ins.then_inc(s["qk"], 1)
                    for tt in range(4):
                        vidx = 4 * w + tt
                        if vidx >= 2:
                            t.wait_ge(s["ve"], vidx - 1)
                        for d in range(16):
                            ins = t.matmul(
                                v_ps[vidx % 2],
                                xT[w % 2][:, WIN * d + 128 * tt:WIN * d + 128 * (tt + 1)],
                                wq_sb[:, 768 * d + 512:768 * (d + 1)],
                                start=(d == 0), stop=(d == 15),
                            )
                        ins.then_inc(s["v"], 1)
                t.wait_ge(s["qke"], 64)
                t.wait_ge(s["ve"], 64)

                def emit_S(sw, pr):
                    h, qb, npair, pb = SWEEPS[sw]
                    pg = pb + pr
                    if pg >= 2:
                        t.wait_ge(s["exp"], pg - 1)
                    for half in range(2):
                        c = 2 * pr + half
                        ins = t.matmul(
                            S2[pg % 2][:, 512 * half:512 * (half + 1)],
                            KT[:, SEQ * h + 128 * c:SEQ * h + 128 * (c + 1)],
                            QT[:, SEQ * h + QB * qb:SEQ * h + QB * (qb + 1)],
                            start=True, stop=True,
                        )
                    ins.then_inc(s["S"], 1)

                def emit_PV(sw, pr):
                    h, qb, npair, pb = SWEEPS[sw]
                    pg = pb + pr
                    nk = 2 * npair
                    diag = pr >= npair - 2
                    if diag:
                        mid = 2 * sw + (pr - (npair - 2))
                        t.wait_ge(s["msk"], mid + 1)
                    else:
                        t.wait_ge(s["exp"], pg + 1)
                    if pr == 0:
                        t.wait_ge(s["ao"], sw)
                    for half in range(2):
                        c = 2 * pr + half
                        for q4 in range(4):
                            ins = t.matmul(
                                att[q4][:, 0:129],
                                PT[pg % 2][:, 512 * half + 128 * q4:512 * half + 128 * (q4 + 1)],
                                Vv[:, NKC * VW * h + VW * c:NKC * VW * h + VW * c + 129],
                                start=(c == 0),
                                stop=(c == nk - 1),
                            )
                    ins.then_inc(s["pv"], 1)

                for sw, (h, qb, npair, pb) in enumerate(SWEEPS):
                    emit_S(sw, 0)
                    if npair > 1:
                        emit_S(sw, 1)
                    for pr in range(npair):
                        if pr + 2 < npair:
                            emit_S(sw, pr + 2)
                        emit_PV(sw, pr)

        with (
            nc.sbuf_tensor("x2", [128, 8 * 2048], F32) as x2,
            nc.sbuf_tensor("ob0", [128, 2048], F32) as ob0,
            nc.sbuf_tensor("ob1", [128, 2048], F32) as ob1,
        ):
            out_sb = [ob0, ob1]

            vsn = VSN

            def vssync(v, ins):
                ins.then_inc(s["vs"], 1)
                vsn[0] += 1
                v.wait_ge(s["vs"], vsn[0])

            def ln_stats_emit(v, ysrc, have_sum=False):
                if not have_sum:
                    v.reduce_sum(stats[:, 0:1], ysrc, axis=AX.X)
                vssync(v, v.scalar_tensor_tensor(
                    ysq[:, :], ysrc, 1.0, ysrc, ALU.mult, ALU.mult,
                    accum_out=stats[:, 1:2],
                ))
                v.tensor_scalar_mul(stats[:, 2:3], stats[:, 0:1], 1.0 / 2048.0)
                vssync(v, v.tensor_scalar_mul(stats[:, 3:4], stats[:, 1:2], 1.0 / 2048.0))
                vssync(v, v.tensor_tensor(stats[:, 4:5], stats[:, 2:3], stats[:, 2:3],
                                          ALU.mult))
                v.tensor_tensor(stats[:, 5:6], stats[:, 3:4], stats[:, 4:5],
                                ALU.subtract).then_inc(s["var"], 1)

            lnx_n = [0]

            def rstd_emit(sc):
                lnx_n[0] += 1
                sc.activation(out=stats[:, 6:7], in_=stats[:, 5:6],
                              func=AF.Ln, bias=f32(1e-5)).then_inc(s["lnx"], 1)
                sc.wait_ge(s["lnx"], lnx_n[0])
                sc.activation(out=stats[:, 7:8], in_=stats[:, 6:7],
                              func=AF.Exp, scale=f32(-0.5)).then_inc(s["rstd"], 1)

            with (
                nc.sbuf_tensor("attnT", [128, 16 * 512], BF16) as attnT,
                nc.sbuf_tensor("wout_sb", [128, 16 * 2048], BF16) as wout_sb,
                nc.sbuf_tensor("xres0", [128, 2048], F32) as xres0,
                nc.sbuf_tensor("xres1", [128, 2048], F32) as xres1,
                nc.sbuf_tensor("y0", [128, 2048], F32) as y0,
                nc.sbuf_tensor("y1", [128, 2048], F32) as y1,
                nc.psum_tensor("o2a", [128, 2048], F32) as o2a,
                nc.psum_tensor("o2b", [128, 2048], F32) as o2b,
                nc.Block() as bA,
            ):
                xres_sb = [xres0, xres1]
                y_sb = [y0, y1]
                o2 = [o2a, o2b]

                @bA.gpsimd
                def _(g):
                    g.wait_ge(s["wco"], 16)
                    g.dma_start(out=wout_sb[:, :], in_=wo_b).then_inc(s["wos"], 16)
                    for gt in range(2):
                        g.dma_start(
                            out=xres_sb[gt % 2][:, :],
                            in_=xres[128 * gt:128 * (gt + 1), :],
                        ).then_inc(s["xr0" if gt % 2 == 0 else "xr1"], 16)
                    pid = nc.partition_id(engines=[mybir.EngineType.Pool])
                    g.wait_ge(s["cc"], pid + 1)
                    flat = ag_out.rearrange("a b c -> (a b) c")
                    for r in range(NC_):
                        g.dma_start(
                            out=attn_my[TOKS * r:TOKS * (r + 1), :],
                            in_=flat[bass.ds(pid * (NC_ * TOKS) + TOKS * r, TOKS), :],
                        ).then_inc(s["amy" + str(r)], 16)
                    for gt in range(2, 8):
                        g.wait_ge(s["psfree"], gt - 1)
                        g.dma_start(
                            out=xres_sb[gt % 2][:, :],
                            in_=xres[128 * gt:128 * (gt + 1), :],
                        ).then_inc(s["xr0" if gt % 2 == 0 else "xr1"], 16)

                @bA.scalar
                def _(sc):
                    for ch in range(16):
                        sc.wait_ge(s["amy" + str(ch // 2)], 16)
                        sc.dma_start_transpose(
                            attnT[:, 512 * ch:512 * (ch + 1)],
                            attn_my[TOKS * (ch // 2):TOKS * (ch // 2) + 512,
                                    128 * (ch % 2):128 * (ch % 2) + 128],
                        ).then_inc(s["aT" + str(ch // 4)], 16)
                    for gt in range(3):
                        sc.wait_ge(s["var"], gt + 1)
                        rstd_emit(sc)
                    for ch in range(16):
                        if ch < 15:
                            sc.wait_ge(s["o2c"], ch + 1)
                        else:
                            sc.wait_ge(s["o2"], 4)
                        sc.dma_start_transpose(
                            attnT[:, 512 * ch:512 * (ch + 1)],
                            attn_my[TOKS * (ch // 2) + QB:
                                    TOKS * (ch // 2) + QB + 512,
                                    128 * (ch % 2):128 * (ch % 2) + 128],
                        ).then_inc(s["aT" + str(ch // 4)], 16)
                    for gt in range(3, 8):
                        sc.wait_ge(s["var"], gt + 1)
                        rstd_emit(sc)

                @bA.vector
                def _(v):
                    for gt in range(8):
                        v.wait_ge(s["o2"], gt + 1)
                        v.wait_ge(s["xr0" if gt % 2 == 0 else "xr1"], 16 * (gt // 2 + 1))
                        v.scalar_tensor_tensor(
                            y_sb[gt % 2][:, :], o2[gt % 2][:, :], 1.0,
                            xres_sb[gt % 2][:, :], ALU.mult, ALU.add,
                            accum_out=stats[:, 0:1],
                        ).then_inc(s["psfree"], 1)
                        v.wait_ge(s["psfree"], gt + 1)
                        ln_stats_emit(v, y_sb[gt % 2][:, :], have_sum=True)
                        v.wait_ge(s["rstd"], gt + 1)
                        v.tensor_scalar(
                            x2[:, 2048 * gt:2048 * (gt + 1)], y_sb[gt % 2][:, :],
                            stats[:, 2:3], stats[:, 7:8], ALU.subtract, ALU.mult,
                        ).then_inc(s["x2"], 1)

                @bA.tensor
                def _(t):
                    t.wait_ge(s["wos"], 16)
                    for gt in range(8):
                        tc, tt = gt // 4, gt % 4
                        if gt >= 2:
                            t.wait_ge(s["psfree"], gt - 1)
                        for ch in range(16):
                            if (gt == 0 or gt == 4) and ch % 4 == 0:
                                t.wait_ge(s["aT" + str(ch // 4)], 64 * (tc + 1))
                            for j in range(4):
                                ins = t.matmul(
                                    o2[gt % 2][:, 512 * j:512 * (j + 1)],
                                    attnT[:, 512 * ch + 128 * tt:512 * ch + 128 * (tt + 1)],
                                    wout_sb[:, 2048 * ch + 512 * j:2048 * ch + 512 * (j + 1)],
                                    start=(ch == 0), stop=(ch == 15),
                                )
                                if gt == 3 and j == 3 and ch < 15:
                                    ins.then_inc(s["o2c"], 1)
                        ins.then_inc(s["o2"], 1)

            for tc in range(2):
                with (
                    nc.sbuf_tensor(f"gatedT_{tc}", [128, 64 * 512], BF16) as gatedT,
                    nc.sbuf_tensor(f"gsil0_{tc}", [128, 512], BF16) as gsil0,
                    nc.sbuf_tensor(f"gsil1_{tc}", [128, 512], BF16) as gsil1,
                ):
                    gsil = [gsil0, gsil1]
                    with (
                        nc.sbuf_tensor(f"x2T_{tc}", [128, 16 * 512], BF16) as x2T,
                        nc.sbuf_tensor(f"guw0_{tc}", [128, 2 * 16 * 256], BF16) as guw0,
                        nc.sbuf_tensor(f"guw1_{tc}", [128, 2 * 16 * 256], BF16) as guw1,
                        nc.psum_tensor(f"tp0_{tc}", [128, 128], F32) as tp0,
                        nc.psum_tensor(f"tp1_{tc}", [128, 128], F32) as tp1,
                        nc.psum_tensor(f"tp2_{tc}", [128, 128], F32) as tp2,
                        nc.psum_tensor(f"tp3_{tc}", [128, 128], F32) as tp3,
                        nc.psum_tensor(f"gA0_{tc}", [128, 512], F32) as gA0,
                        nc.psum_tensor(f"gA1_{tc}", [128, 512], F32) as gA1,
                        nc.psum_tensor(f"uA0_{tc}", [128, 512], F32) as uA0,
                        nc.psum_tensor(f"uA1_{tc}", [128, 512], F32) as uA1,
                        nc.Block() as bB,
                    ):
                        guw = [guw0, guw1]
                        gA = [gA0, gA1]
                        uA = [uA0, uA1]
                        tp_ps = [tp0, tp1, tp2, tp3]

                        @bB.gpsimd
                        def _(g, tc=tc):
                            if tc == 0:
                                g.wait_ge(s["o2"], 8)
                                g.wait_ge(s["x2"], 8)
                            else:
                                g.wait_ge(s["d3e"], 64)
                            for hb in range(32):
                                ghb = 32 * tc + hb
                                if ghb >= 2:
                                    g.wait_ge(s["gted"], 2 * (ghb - 1))
                                gp = "guw0" if ghb % 2 == 0 else "guw1"
                                g.dma_start(
                                    out=guw[ghb % 2][:, 0:4096].rearrange("p (d c) -> p d c", d=16),
                                    in_=wgT[hb].rearrange("d p c -> p d c"),
                                ).then_inc(s[gp], 16)
                                g.dma_start(
                                    out=guw[ghb % 2][:, 4096:8192].rearrange("p (d c) -> p d c", d=16),
                                    in_=wuT[hb].rearrange("d p c -> p d c"),
                                ).then_inc(s[gp], 16)

                        @bB.scalar
                        def _(sc, tc=tc):
                            for k in range(64):
                                tj = 64 * tc + k
                                sc.wait_ge(s["tp"], tj + 1)
                                tt, j = k // 16, k % 16
                                sc.copy(
                                    x2T[:, 512 * j + 128 * tt:512 * j + 128 * tt + 128],
                                    tp_ps[tj % 4][:, :],
                                ).then_inc(s["tpe"], 1)
                            for gct0 in range(64):
                                gct = 64 * tc + gct0
                                sc.wait_ge(s["g"], gct + 1)
                                if gct >= 2:
                                    sc.wait_ge(s["gted"], gct - 1)
                                sc.activation(out=gsil[gct % 2][:, :], in_=gA[gct % 2][:, :],
                                              func=AF.Silu).then_inc(s["sil"], 1)
                            if tc == 1:
                                for tt in range(4):
                                    sc.wait_ge(s["var"], 8 + tt + 1)
                                    rstd_emit(sc)

                        @bB.vector
                        def _(v, tc=tc):
                            for gct0 in range(64):
                                gct = 64 * tc + gct0
                                H = gct0
                                v.wait_ge(s["sil"], gct + 1)
                                v.wait_ge(s["u"], gct + 1)
                                v.tensor_tensor(
                                    gatedT[:, 512 * H:512 * (H + 1)],
                                    gsil[gct % 2][:, :], uA[gct % 2][:, :], ALU.mult,
                                ).then_inc(s["gted"], 1)
                            if tc == 1:
                                v.wait_ge(s["d3e"], 64)
                                for tt in range(4):
                                    gt = tt
                                    ln_stats_emit(v, x2[:, 2048 * gt:2048 * (gt + 1)])
                                    v.wait_ge(s["rstd"], 8 + tt + 1)
                                    if gt >= 2:
                                        v.wait_ge(s["outd0" if gt % 2 == 0 else "outd1"],
                                                  16 * (gt // 2))
                                    v.tensor_scalar(
                                        out_sb[gt % 2][:, :], x2[:, 2048 * gt:2048 * (gt + 1)],
                                        stats[:, 2:3], stats[:, 7:8], ALU.subtract, ALU.mult,
                                    ).then_inc(s["ob"], 1)

                        @bB.sync
                        def _(sy, tc=tc):
                            if tc == 1:
                                for tt in range(4):
                                    gt = tt
                                    sy.wait_ge(s["ob"], gt + 1)
                                    sy.dma_start(
                                        out=out[128 * gt:128 * (gt + 1), :],
                                        in_=out_sb[gt % 2][:, :],
                                    ).then_inc(s["outd0" if gt % 2 == 0 else "outd1"], 16)

                        @bB.tensor
                        def _(t, tc=tc):
                            for k in range(64):
                                tj = 64 * tc + k
                                tt, j = k // 16, k % 16
                                gt = 4 * tc + tt
                                t.wait_ge(s["x2"], gt + 1)
                                if tj >= 4:
                                    t.wait_ge(s["tpe"], tj - 3)
                                t.transpose(tp_ps[tj % 4][:, :],
                                            x2[:, 2048 * gt + 128 * j:2048 * gt + 128 * (j + 1)],
                                            identf_sb[:, :]).then_inc(s["tp"], 1)
                            t.wait_ge(s["tpe"], 64 * (tc + 1))
                            for hb in range(32):
                                ghb = 32 * tc + hb
                                t.wait_ge(s["guw0" if ghb % 2 == 0 else "guw1"],
                                          32 * (ghb // 2 + 1))
                                for ht in range(2):
                                    gct = 64 * tc + 2 * hb + ht
                                    if gct >= 2:
                                        t.wait_ge(s["sil"], gct - 1)
                                    for d in range(16):
                                        ins = t.matmul(
                                            gA[gct % 2][:, :],
                                            guw[ghb % 2][:, 256 * d + 128 * ht:256 * d + 128 * (ht + 1)],
                                            x2T[:, 512 * d:512 * (d + 1)],
                                            start=(d == 0), stop=(d == 15),
                                        )
                                    ins.then_inc(s["g"], 1)
                                    if gct >= 2:
                                        t.wait_ge(s["gted"], gct - 1)
                                    for d in range(16):
                                        ins = t.matmul(
                                            uA[gct % 2][:, :],
                                            guw[ghb % 2][:, 4096 + 256 * d + 128 * ht:
                                                          4096 + 256 * d + 128 * (ht + 1)],
                                            x2T[:, 512 * d:512 * (d + 1)],
                                            start=(d == 0), stop=(d == 15),
                                        )
                                    ins.then_inc(s["u"], 1)

                    with (
                        nc.sbuf_tensor(f"wdT0_{tc}", [128, 8 * 1024], BF16) as wdT0,
                        nc.sbuf_tensor(f"wdT1_{tc}", [128, 8 * 1024], BF16) as wdT1,
                        nc.sbuf_tensor(f"p3_{tc}", [128, 4 * 1024], F32) as p3,
                        nc.psum_tensor(f"d3_{tc}", [128, 2048], F32) as d3,
                        nc.Block() as bC,
                    ):
                        wdT_sb = [wdT0, wdT1]

                        @bC.gpsimd
                        def _(g, tc=tc):
                            g.wait_ge(s["wcd"], 16 * 16)
                            for dh in range(2):
                                for ho in range(8):
                                    gb = 16 * tc + 8 * dh + ho
                                    if gb >= 2:
                                        g.wait_ge(s["d3"], 4 * (gb - 1))
                                    g.dma_start(
                                        out=wdT_sb[gb % 2][:, :].rearrange("p (k c) -> p k c", k=8),
                                        in_=wdT_b[dh, ho].rearrange("k p c -> p k c"),
                                    ).then_inc(s["wd0" if gb % 2 == 0 else "wd1"], 16)

                        @bC.scalar
                        def _(sc, tc=tc):
                            if tc == 1:
                                for tt in range(4):
                                    sc.wait_ge(s["var"], 12 + tt + 1)
                                    rstd_emit(sc)

                        @bC.vector
                        def _(v, tc=tc):
                            for dh in range(2):
                                for ho in range(8):
                                    for tt in range(4):
                                        gt = 4 * tc + tt
                                        dct = 64 * tc + 32 * dh + 4 * ho + tt
                                        v.wait_ge(s["d3"], dct + 1)
                                        dps = d3[:, 1024 * (tt % 2):1024 * (tt % 2 + 1)]
                                        x2sl = x2[:, 2048 * gt + 1024 * dh:
                                                  2048 * gt + 1024 * (dh + 1)]
                                        p3sl = p3[:, 1024 * tt:1024 * (tt + 1)]
                                        if ho == 0:
                                            v.tensor_add(p3sl, dps, x2sl).then_inc(s["d3e"], 1)
                                        elif ho < 7:
                                            v.tensor_add(p3sl, dps, p3sl).then_inc(s["d3e"], 1)
                                        else:
                                            v.tensor_add(x2sl, dps, p3sl).then_inc(s["d3e"], 1)
                            if tc == 1:
                                v.wait_ge(s["d3e"], 128)
                                for tt in range(4):
                                    gt = 4 + tt
                                    ln_stats_emit(v, x2[:, 2048 * gt:2048 * (gt + 1)])
                                    v.wait_ge(s["rstd"], 12 + tt + 1)
                                    v.wait_ge(s["outd0" if gt % 2 == 0 else "outd1"],
                                              16 * (gt // 2))
                                    v.tensor_scalar(
                                        out_sb[gt % 2][:, :], x2[:, 2048 * gt:2048 * (gt + 1)],
                                        stats[:, 2:3], stats[:, 7:8], ALU.subtract, ALU.mult,
                                    ).then_inc(s["ob"], 1)

                        @bC.sync
                        def _(sy, tc=tc):
                            if tc == 1:
                                for tt in range(4):
                                    gt = 4 + tt
                                    sy.wait_ge(s["ob"], gt + 1)
                                    sy.dma_start(
                                        out=out[128 * gt:128 * (gt + 1), :],
                                        in_=out_sb[gt % 2][:, :],
                                    ).then_inc(s["outd0" if gt % 2 == 0 else "outd1"], 16)
                                sy.wait_ge(s["outd0"], 16 * 4)
                                sy.wait_ge(s["outd1"], 16 * 4)

                        @bC.tensor
                        def _(t, tc=tc):
                            for dh in range(2):
                                for ho in range(8):
                                    gb = 16 * tc + 8 * dh + ho
                                    t.wait_ge(s["wd0" if gb % 2 == 0 else "wd1"],
                                              16 * (gb // 2 + 1))
                                    if dh == 0 and ho == 0:
                                        t.wait_ge(s["gted"], 64 * (tc + 1))
                                    for tt in range(4):
                                        dct = 64 * tc + 32 * dh + 4 * ho + tt
                                        if dct >= 2:
                                            t.wait_ge(s["d3e"], dct - 1)
                                        for hc in range(8):
                                            H = 8 * ho + hc
                                            for j in range(2):
                                                ins = t.matmul(
                                                    d3[:, 1024 * (tt % 2) + 512 * j:
                                                       1024 * (tt % 2) + 512 * (j + 1)],
                                                    gatedT[:, 512 * H + 128 * tt:
                                                           512 * H + 128 * (tt + 1)],
                                                    wdT_sb[gb % 2][:, 1024 * hc + 512 * j:
                                                                   1024 * hc + 512 * (j + 1)],
                                                    start=(hc == 0), stop=(hc == 7),
                                                )
                                        ins.then_inc(s["d3"], 1)

    return nc


_NC_CACHE = None


def _get_nc():
    global _NC_CACHE
    if _NC_CACHE is None:
        _NC_CACHE = build_nc()
    return _NC_CACHE


def kernel(L0_attn_x, L0_attn_w_qkv, L0_attn_w_out, L0_ffn_w_gate,
           L0_ffn_w_up, L0_ffn_w_down):
    x = np.ascontiguousarray(np.asarray(L0_attn_x, dtype=np.float32).reshape(SEQ, DM))
    w_qkv = np.asarray(L0_attn_w_qkv, dtype=np.float32)
    w_out = np.asarray(L0_attn_w_out, dtype=np.float32)
    w_g = np.asarray(L0_ffn_w_gate, dtype=np.float32)
    w_u = np.asarray(L0_ffn_w_up, dtype=np.float32)
    w_d = np.asarray(L0_ffn_w_down, dtype=np.float32)

    xq = np.ascontiguousarray(x.reshape(NWIN, WIN, 16, 128).transpose(0, 2, 3, 1))
    wo = np.ascontiguousarray(
        w_out.reshape(16, 128, 2048).transpose(1, 0, 2).reshape(128, 16 * 2048))
    wgT = np.ascontiguousarray(w_g.reshape(32, 256, 16, 128).transpose(0, 2, 3, 1))
    wuT = np.ascontiguousarray(w_u.reshape(32, 256, 16, 128).transpose(0, 2, 3, 1))
    wdT = np.ascontiguousarray(
        w_d.reshape(2, 1024, 8, 8, 128).transpose(0, 2, 3, 4, 1))

    in_maps = []
    for c in range(NC_):
        h0, h1 = 2 * c, 2 * c + 1
        blocks = [w_qkv[:, h0 * 128:(h0 + 1) * 128],
                  w_qkv[:, h1 * 128:(h1 + 1) * 128],
                  w_qkv[:, 2048 + h0 * 128:2048 + (h0 + 1) * 128],
                  w_qkv[:, 2048 + h1 * 128:2048 + (h1 + 1) * 128],
                  w_qkv[:, 4096 + h0 * 128:4096 + (h0 + 1) * 128],
                  w_qkv[:, 4096 + h1 * 128:4096 + (h1 + 1) * 128]]
        wq_c = np.stack(blocks, axis=1)
        wq_c = np.ascontiguousarray(
            wq_c.reshape(16, 128, 6, 128).transpose(1, 0, 2, 3).reshape(128, 16 * 768))
        in_maps.append({
            "xq": xq, "wq": wq_c, "wo": wo, "wgT": wgT, "wuT": wuT,
            "wdT": wdT, "xres": np.ascontiguousarray(x[TOKS * c:TOKS * (c + 1)]),
        })

    nc = _get_nc()
    res = run_bass_kernel_spmd(nc, in_maps, core_ids=list(range(NC_)))
    out = np.concatenate([r["out"] for r in res.results], axis=0)
    return out.reshape(1, SEQ, DM)


# version 25 (final)
# speedup vs baseline: 1.0141x; 1.0083x over previous
import numpy as np
import concourse.bass as bass
import concourse.mybir as mybir
from concourse.bass_utils import run_bass_kernel_spmd

import ml_dtypes
BFNP = ml_dtypes.bfloat16

F32 = mybir.dt.float32
BF16 = mybir.dt.bfloat16
AF = mybir.ActivationFunctionType
ALU = mybir.AluOpType
AX = mybir.AxisListType
f32 = np.float32

SEQ = 8192
DM = 2048
NC_ = 8
TOKS = SEQ // NC_
NWIN = 16
WIN = 512
NQB = 16
QB = 512
CEXP = f32(5.0)
SSCL = f32(1.0 / np.sqrt(128.0))
VW = 132
NKC = 64


def _sweeps():
    out = []
    pb = 0
    for qb in range(NQB):
        for h in range(2):
            npair = 2 * (qb + 1)
            out.append((h, qb, npair, pb))
            pb += npair
    return out


SWEEPS = _sweeps()


def build_nc():
    nc = bass.Bass("TRN2", target_bir_lowering=False, debug=False, num_devices=NC_,
                   detect_race_conditions=True)

    xq = nc.dram_tensor("xq", [NWIN, 16, 128, WIN], F32, kind="ExternalInput").ap()
    wq = nc.dram_tensor("wq", [128, 16 * 768], F32, kind="ExternalInput").ap()
    wo = nc.dram_tensor("wo", [128, 16 * 2048], F32, kind="ExternalInput").ap()
    wgT = nc.dram_tensor("wgT", [32, 16, 128, 256], F32, kind="ExternalInput").ap()
    wuT = nc.dram_tensor("wuT", [32, 16, 128, 256], F32, kind="ExternalInput").ap()
    wdT = nc.dram_tensor("wdT", [2, 8, 8, 128, 1024], F32, kind="ExternalInput").ap()
    xres = nc.dram_tensor("xres", [TOKS, DM], F32, kind="ExternalInput").ap()
    out = nc.dram_tensor("out", [TOKS, DM], F32, kind="ExternalOutput").ap()

    ag_in = nc.dram_tensor("ag_in", [SEQ, 256], BF16, kind="Internal").ap()
    ag_out = nc.dram_tensor("ag_out", [16, NC_ * 512, 256], BF16, kind="Internal").ap()
    wo_b = nc.dram_tensor("wo_b", [128, 16 * 2048], BF16, kind="Internal").ap()
    wdT_b = nc.dram_tensor("wdT_b", [2, 8, 8, 128, 1024], BF16, kind="Internal").ap()
    attn_my = nc.dram_tensor("attn_my", [NC_ * TOKS, 256], BF16, kind="Internal").ap()

    ident_h = nc.inline_tensor(np.eye(128, dtype=BFNP), "ident")
    identf_h = nc.inline_tensor(np.eye(128, dtype=np.float32), "identf")
    cm = np.zeros((128, 4 * 512), dtype=np.float32)
    for o4 in range(4):
        for p in range(128):
            j0 = 128 * o4 + p
            if j0 < 512:
                cm[p, 512 * o4 + j0:512 * (o4 + 1)] = 1.0
    cmask_h = nc.inline_tensor(cm.astype(BFNP), "cmask")

    names = [
        "xt0", "xt1", "wqs", "qk", "qke", "v", "ve",
        "S", "exp", "msk", "pv", "ao", "slab0", "slab1", "slab2", "slab3", "cc", "o2c", "amy0", "amy1", "amy2", "amy3", "amy4", "amy5", "amy6", "amy7",
        "wos", "aT0", "aT1", "aT2", "aT3", "xr0", "xr1", "o2", "psfree", "var", "rstd", "x2",
        "tp", "tpe", "guw0", "guw1", "g", "u", "sil", "gted", "wd0", "wd1",
        "d3", "d3e", "ob", "outd0", "outd1", "cst", "lnx", "vs", "wco", "wcd", "ccw",
    ]

    VSN = [0]
    import contextlib
    stack = contextlib.ExitStack()
    with stack:
        s = {n: stack.enter_context(nc.semaphore("s_" + n)) for n in names}

        ident_sb = stack.enter_context(nc.sbuf_tensor("ident_sb", [128, 128], BF16))
        identf_sb = stack.enter_context(nc.sbuf_tensor("identf_sb", [128, 128], F32))
        cmask_sb = stack.enter_context(nc.sbuf_tensor("cmask_sb", [128, 2048], BF16))
        stats = stack.enter_context(nc.sbuf_tensor("stats", [128, 8], F32))
        linv = stack.enter_context(nc.sbuf_tensor("linv", [128, 4], F32))
        ysq = stack.enter_context(nc.sbuf_tensor("ysq", [128, 2048], F32))

        with (
            nc.sbuf_tensor("QT", [128, 2 * SEQ], BF16) as QT,
            nc.sbuf_tensor("KT", [128, 2 * SEQ], BF16) as KT,
            nc.sbuf_tensor("Vv", [128, 2 * NKC * VW], BF16) as Vv,
            nc.sbuf_tensor("wq_sb", [128, 16 * 768], BF16) as wq_sb,
            nc.sbuf_tensor("xT0", [128, 16 * WIN], BF16) as xT0,
            nc.sbuf_tensor("xT1", [128, 16 * WIN], BF16) as xT1,
            nc.sbuf_tensor("PT0", [128, 1024], BF16) as PT0,
            nc.sbuf_tensor("PT1", [128, 1024], BF16) as PT1,
            nc.sbuf_tensor("ao0", [128, 512], BF16) as ao0,
            nc.sbuf_tensor("ao1", [128, 512], BF16) as ao1,
            nc.sbuf_tensor("ao2", [128, 512], BF16) as ao2,
            nc.sbuf_tensor("ao3", [128, 512], BF16) as ao3,
            nc.psum_tensor("S0", [128, 1024], F32) as S0,
            nc.psum_tensor("S1", [128, 1024], F32) as S1,
            nc.psum_tensor("att0", [128, 129], F32) as att0,
            nc.psum_tensor("att1", [128, 129], F32) as att1,
            nc.psum_tensor("att2", [128, 129], F32) as att2,
            nc.psum_tensor("att3", [128, 129], F32) as att3,
            nc.Block() as b1,
        ):
            vsn = VSN
            xT = [xT0, xT1]
            PT = [PT0, PT1]
            ao = [ao0, ao1, ao2, ao3]
            S2 = [S0, S1]
            att = [att0, att1, att2, att3]
            qk_ps = [S0[:, 0:512], S0[:, 512:1024]]
            v_ps = [S1[:, 0:256], S1[:, 512:768]]

            @b1.gpsimd
            def _(g):
                g.dma_start(out=wq_sb[:, :], in_=wq).then_inc(s["wqs"], 16)
                for w in range(NWIN):
                    if w >= 2:
                        g.wait_ge(s["v"], 4 * (w - 1))
                    g.dma_start(
                        out=xT[w % 2][:, :].rearrange("p (d c) -> p d c", d=16),
                        in_=xq[w].rearrange("d p c -> p d c"),
                    ).then_inc(s["xt0" if w % 2 == 0 else "xt1"], 16)
                g.dma_start(out=wo_b, in_=wo).then_inc(s["wco"], 16)
                wdcast = [(dh, ho) for dh in range(2) for ho in range(8)]
                for k in range(16):
                    n_done = 2 * k + 2
                    for par in range(4):
                        cnt = max(0, (n_done - par + 3) // 4)
                        if cnt:
                            g.wait_ge(s["slab" + str(par)], 16 * cnt)
                    g.collective_compute(
                        "AllGather", ALU.bypass,
                        replica_groups=[list(range(NC_))],
                        ins=[ag_in[512 * k:512 * (k + 1), :]], outs=[ag_out[k]],
                    ).then_inc(s["cc"], 1)
                    for dh, ho in wdcast[k:k + 1]:
                        g.dma_start(out=wdT_b[dh, ho], in_=wdT[dh, ho]).then_inc(s["wcd"], 16)


            @b1.sync
            def _(sy):
                sy.dma_start(out=ident_sb[:, :], in_=ident_h.ap()).then_inc(s["cst"], 16)
                sy.dma_start(out=identf_sb[:, :], in_=identf_h.ap()).then_inc(s["cst"], 16)
                sy.dma_start(out=cmask_sb[:, :], in_=cmask_h.ap()).then_inc(s["cst"], 16)
                for sw, (h, qb, npair, pb) in enumerate(SWEEPS):
                    sy.wait_ge(s["ao"], sw + 1)
                    sy.dma_start(
                        out=ag_in.rearrange("(q t p) c -> q p t c", t=4, p=128)[
                            qb, :, :, 128 * h:128 * (h + 1)],
                        in_=ao[sw % 4][:, :].rearrange("p (t c) -> p t c", t=4),
                    ).then_inc(s["slab" + str(sw % 4)], 16)

            @b1.vector
            def _(v):
                vv4 = Vv[:, :].rearrange("p (h c w) -> p h c w", h=2, c=NKC, w=VW)
                v.memset(vv4[:, 0, :, 128:129], 1.0)
                v.memset(vv4[:, 1, :, 128:129], 1.0)
                for w in range(NWIN):
                    for ct in range(4):
                        gidx = 4 * w + ct
                        v.wait_ge(s["qk"], gidx + 1)
                        dst = QT if ct < 2 else KT
                        hh = ct % 2
                        v.tensor_copy(
                            dst[:, SEQ * hh + WIN * w: SEQ * hh + WIN * (w + 1)],
                            qk_ps[gidx % 2],
                        ).then_inc(s["qke"], 1)
                    for tt in range(4):
                        vidx = 4 * w + tt
                        cidx = 4 * w + tt
                        v.wait_ge(s["v"], vidx + 1)
                        v.tensor_copy(
                            Vv[:, VW * cidx: VW * cidx + 128],
                            v_ps[vidx % 2][:, 0:128],
                        )
                        v.tensor_copy(
                            Vv[:, NKC * VW + VW * cidx: NKC * VW + VW * cidx + 128],
                            v_ps[vidx % 2][:, 128:256],
                        ).then_inc(s["ve"], 1)
                v.wait_ge(s["cst"], 48)
                for sw, (h, qb, npair, pb) in enumerate(SWEEPS):
                    for which in range(2):
                        pr = npair - 2 + which
                        pg = pb + pr
                        v.wait_ge(s["exp"], pg + 1)
                        for half in range(2):
                            o4 = 2 * which + half
                            ins = v.tensor_tensor(
                                PT[pg % 2][:, 512 * half:512 * (half + 1)],
                                PT[pg % 2][:, 512 * half:512 * (half + 1)],
                                cmask_sb[:, 512 * o4:512 * (o4 + 1)],
                                ALU.mult,
                            )
                            if half == 1:
                                ins.then_inc(s["msk"], 1)
                    v.wait_ge(s["pv"], pb + npair)
                    if sw >= 4:
                        v.wait_ge(s["slab" + str(sw % 4)], 16 * (sw // 4))
                    for t in range(4):
                        ins = v.reciprocal(
                            linv[:, t:t + 1],
                            att[t][:, 128:129],
                        )
                    ins.then_inc(s["vs"], 1)
                    vsn[0] += 1
                    v.wait_ge(s["vs"], vsn[0])
                    for t in range(4):
                        ins = v.tensor_scalar_mul(
                            ao[sw % 4][:, 128 * t:128 * (t + 1)],
                            att[t][:, 0:128],
                            linv[:, t:t + 1],
                        )
                        if t == 3:
                            ins.then_inc(s["ao"], 1)

            @b1.scalar
            def _(sc):
                for sw, (h, qb, npair, pb) in enumerate(SWEEPS):
                    for pr in range(npair):
                        pg = pb + pr
                        sc.wait_ge(s["S"], pg + 1)
                        if pg >= 2:
                            sc.wait_ge(s["pv"], pg - 1)
                        sc.activation(
                            out=PT[pg % 2][:, :], in_=S2[pg % 2][:, :],
                            func=AF.Exp, bias=-CEXP, scale=SSCL,
                        ).then_inc(s["exp"], 1)

            @b1.tensor
            def _(t):
                t.wait_ge(s["wqs"], 16)
                for w in range(NWIN):
                    t.wait_ge(s["xt0" if w % 2 == 0 else "xt1"], 16 * (w // 2 + 1))
                    for ct in range(4):
                        gidx = 4 * w + ct
                        if gidx >= 2:
                            t.wait_ge(s["qke"], gidx - 1)
                        for d in range(16):
                            ins = t.matmul(
                                qk_ps[gidx % 2],
                                wq_sb[:, 768 * d + 128 * ct:768 * d + 128 * (ct + 1)],
                                xT[w % 2][:, WIN * d:WIN * (d + 1)],
                                start=(d == 0), stop=(d == 15),
                            )
                        ins.then_inc(s["qk"], 1)
                    for tt in range(4):
                        vidx = 4 * w + tt
                        if vidx >= 2:
                            t.wait_ge(s["ve"], vidx - 1)
                        for d in range(16):
                            ins = t.matmul(
                                v_ps[vidx % 2],
                                xT[w % 2][:, WIN * d + 128 * tt:WIN * d + 128 * (tt + 1)],
                                wq_sb[:, 768 * d + 512:768 * (d + 1)],
                                start=(d == 0), stop=(d == 15),
                            )
                        ins.then_inc(s["v"], 1)
                t.wait_ge(s["qke"], 64)
                t.wait_ge(s["ve"], 64)

                def emit_S(sw, pr):
                    h, qb, npair, pb = SWEEPS[sw]
                    pg = pb + pr
                    if pg >= 2:
                        t.wait_ge(s["exp"], pg - 1)
                    for half in range(2):
                        c = 2 * pr + half
                        ins = t.matmul(
                            S2[pg % 2][:, 512 * half:512 * (half + 1)],
                            KT[:, SEQ * h + 128 * c:SEQ * h + 128 * (c + 1)],
                            QT[:, SEQ * h + QB * qb:SEQ * h + QB * (qb + 1)],
                            start=True, stop=True,
                        )
                    ins.then_inc(s["S"], 1)

                def emit_PV(sw, pr):
                    h, qb, npair, pb = SWEEPS[sw]
                    pg = pb + pr
                    nk = 2 * npair
                    diag = pr >= npair - 2
                    if diag:
                        mid = 2 * sw + (pr - (npair - 2))
                        t.wait_ge(s["msk"], mid + 1)
                    else:
                        t.wait_ge(s["exp"], pg + 1)
                    if pr == 0:
                        t.wait_ge(s["ao"], sw)
                    for half in range(2):
                        c = 2 * pr + half
                        for q4 in range(4):
                            ins = t.matmul(
                                att[q4][:, 0:129],
                                PT[pg % 2][:, 512 * half + 128 * q4:512 * half + 128 * (q4 + 1)],
                                Vv[:, NKC * VW * h + VW * c:NKC * VW * h + VW * c + 129],
                                start=(c == 0),
                                stop=(c == nk - 1),
                            )
                    ins.then_inc(s["pv"], 1)

                for sw, (h, qb, npair, pb) in enumerate(SWEEPS):
                    emit_S(sw, 0)
                    if npair > 1:
                        emit_S(sw, 1)
                    for pr in range(npair):
                        if pr + 2 < npair:
                            emit_S(sw, pr + 2)
                        emit_PV(sw, pr)

        with (
            nc.sbuf_tensor("x2", [128, 8 * 2048], F32) as x2,
            nc.sbuf_tensor("ob0", [128, 2048], F32) as ob0,
            nc.sbuf_tensor("ob1", [128, 2048], F32) as ob1,
        ):
            out_sb = [ob0, ob1]

            vsn = VSN

            def vssync(v, ins):
                ins.then_inc(s["vs"], 1)
                vsn[0] += 1
                v.wait_ge(s["vs"], vsn[0])

            def ln_stats_emit(v, ysrc):
                v.reduce_sum(stats[:, 0:1], ysrc, axis=AX.X)
                vssync(v, v.scalar_tensor_tensor(
                    ysq[:, :], ysrc, 1.0, ysrc, ALU.mult, ALU.mult,
                    accum_out=stats[:, 1:2],
                ))
                v.tensor_scalar_mul(stats[:, 2:3], stats[:, 0:1], 1.0 / 2048.0)
                vssync(v, v.tensor_scalar_mul(stats[:, 3:4], stats[:, 1:2], 1.0 / 2048.0))
                vssync(v, v.tensor_tensor(stats[:, 4:5], stats[:, 2:3], stats[:, 2:3],
                                          ALU.mult))
                v.tensor_tensor(stats[:, 5:6], stats[:, 3:4], stats[:, 4:5],
                                ALU.subtract).then_inc(s["var"], 1)

            lnx_n = [0]

            def rstd_emit(sc):
                lnx_n[0] += 1
                sc.activation(out=stats[:, 6:7], in_=stats[:, 5:6],
                              func=AF.Ln, bias=f32(1e-5)).then_inc(s["lnx"], 1)
                sc.wait_ge(s["lnx"], lnx_n[0])
                sc.activation(out=stats[:, 7:8], in_=stats[:, 6:7],
                              func=AF.Exp, scale=f32(-0.5)).then_inc(s["rstd"], 1)

            with (
                nc.sbuf_tensor("attnT", [128, 16 * 512], BF16) as attnT,
                nc.sbuf_tensor("wout_sb", [128, 16 * 2048], BF16) as wout_sb,
                nc.sbuf_tensor("xres0", [128, 2048], F32) as xres0,
                nc.sbuf_tensor("xres1", [128, 2048], F32) as xres1,
                nc.sbuf_tensor("y0", [128, 2048], F32) as y0,
                nc.sbuf_tensor("y1", [128, 2048], F32) as y1,
                nc.psum_tensor("o2a", [128, 2048], F32) as o2a,
                nc.psum_tensor("o2b", [128, 2048], F32) as o2b,
                nc.Block() as bA,
            ):
                xres_sb = [xres0, xres1]
                y_sb = [y0, y1]
                o2 = [o2a, o2b]

                @bA.gpsimd
                def _(g):
                    g.wait_ge(s["wco"], 16)
                    g.dma_start(out=wout_sb[:, :], in_=wo_b).then_inc(s["wos"], 16)
                    for gt in range(2):
                        g.dma_start(
                            out=xres_sb[gt % 2][:, :],
                            in_=xres[128 * gt:128 * (gt + 1), :],
                        ).then_inc(s["xr0" if gt % 2 == 0 else "xr1"], 16)
                    pid = nc.partition_id(engines=[mybir.EngineType.Pool])
                    g.wait_ge(s["cc"], pid * 2 + 2)
                    for r in range(NC_):
                        g.dma_start(
                            out=attn_my[TOKS * r:TOKS * (r + 1), :]
                                .rearrange("(h t) c -> h t c", h=2),
                            in_=ag_out[bass.ds(pid * 2, 2),
                                       512 * r:512 * (r + 1), :],
                        ).then_inc(s["amy" + str(r)], 16)
                    for gt in range(2, 8):
                        g.wait_ge(s["psfree"], gt - 1)
                        g.dma_start(
                            out=xres_sb[gt % 2][:, :],
                            in_=xres[128 * gt:128 * (gt + 1), :],
                        ).then_inc(s["xr0" if gt % 2 == 0 else "xr1"], 16)

                @bA.scalar
                def _(sc):
                    for ch in range(16):
                        sc.wait_ge(s["amy" + str(ch // 2)], 16)
                        sc.dma_start_transpose(
                            attnT[:, 512 * ch:512 * (ch + 1)],
                            attn_my[TOKS * (ch // 2):TOKS * (ch // 2) + 512,
                                    128 * (ch % 2):128 * (ch % 2) + 128],
                        ).then_inc(s["aT" + str(ch // 4)], 16)
                    for gt in range(3):
                        sc.wait_ge(s["var"], gt + 1)
                        rstd_emit(sc)
                    for ch in range(16):
                        if ch < 15:
                            sc.wait_ge(s["o2c"], ch + 1)
                        else:
                            sc.wait_ge(s["o2"], 4)
                        sc.dma_start_transpose(
                            attnT[:, 512 * ch:512 * (ch + 1)],
                            attn_my[TOKS * (ch // 2) + QB:
                                    TOKS * (ch // 2) + QB + 512,
                                    128 * (ch % 2):128 * (ch % 2) + 128],
                        ).then_inc(s["aT" + str(ch // 4)], 16)
                    for gt in range(3, 8):
                        sc.wait_ge(s["var"], gt + 1)
                        rstd_emit(sc)

                @bA.vector
                def _(v):
                    for gt in range(8):
                        v.wait_ge(s["o2"], gt + 1)
                        v.wait_ge(s["xr0" if gt % 2 == 0 else "xr1"], 16 * (gt // 2 + 1))
                        v.tensor_add(y_sb[gt % 2][:, :], o2[gt % 2][:, :],
                                     xres_sb[gt % 2][:, :]).then_inc(s["psfree"], 1)
                        v.wait_ge(s["psfree"], gt + 1)
                        ln_stats_emit(v, y_sb[gt % 2][:, :])
                        v.wait_ge(s["rstd"], gt + 1)
                        v.tensor_scalar(
                            x2[:, 2048 * gt:2048 * (gt + 1)], y_sb[gt % 2][:, :],
                            stats[:, 2:3], stats[:, 7:8], ALU.subtract, ALU.mult,
                        ).then_inc(s["x2"], 1)

                @bA.tensor
                def _(t):
                    t.wait_ge(s["wos"], 16)
                    for gt in range(8):
                        tc, tt = gt // 4, gt % 4
                        if gt >= 2:
                            t.wait_ge(s["psfree"], gt - 1)
                        for ch in range(16):
                            if (gt == 0 or gt == 4) and ch % 4 == 0:
                                t.wait_ge(s["aT" + str(ch // 4)], 64 * (tc + 1))
                            for j in range(4):
                                ins = t.matmul(
                                    o2[gt % 2][:, 512 * j:512 * (j + 1)],
                                    attnT[:, 512 * ch + 128 * tt:512 * ch + 128 * (tt + 1)],
                                    wout_sb[:, 2048 * ch + 512 * j:2048 * ch + 512 * (j + 1)],
                                    start=(ch == 0), stop=(ch == 15),
                                )
                                if gt == 3 and j == 3 and ch < 15:
                                    ins.then_inc(s["o2c"], 1)
                        ins.then_inc(s["o2"], 1)

            for tc in range(2):
                with (
                    nc.sbuf_tensor(f"gatedT_{tc}", [128, 64 * 512], BF16) as gatedT,
                    nc.sbuf_tensor(f"gsil0_{tc}", [128, 512], BF16) as gsil0,
                    nc.sbuf_tensor(f"gsil1_{tc}", [128, 512], BF16) as gsil1,
                ):
                    gsil = [gsil0, gsil1]
                    with (
                        nc.sbuf_tensor(f"x2T_{tc}", [128, 16 * 512], BF16) as x2T,
                        nc.sbuf_tensor(f"guw0_{tc}", [128, 2 * 16 * 256], BF16) as guw0,
                        nc.sbuf_tensor(f"guw1_{tc}", [128, 2 * 16 * 256], BF16) as guw1,
                        nc.psum_tensor(f"tp0_{tc}", [128, 128], F32) as tp0,
                        nc.psum_tensor(f"tp1_{tc}", [128, 128], F32) as tp1,
                        nc.psum_tensor(f"tp2_{tc}", [128, 128], F32) as tp2,
                        nc.psum_tensor(f"tp3_{tc}", [128, 128], F32) as tp3,
                        nc.psum_tensor(f"gA0_{tc}", [128, 512], F32) as gA0,
                        nc.psum_tensor(f"gA1_{tc}", [128, 512], F32) as gA1,
                        nc.psum_tensor(f"uA0_{tc}", [128, 512], F32) as uA0,
                        nc.psum_tensor(f"uA1_{tc}", [128, 512], F32) as uA1,
                        nc.Block() as bB,
                    ):
                        guw = [guw0, guw1]
                        gA = [gA0, gA1]
                        uA = [uA0, uA1]
                        tp_ps = [tp0, tp1, tp2, tp3]

                        @bB.gpsimd
                        def _(g, tc=tc):
                            if tc == 0:
                                g.wait_ge(s["o2"], 8)
                                g.wait_ge(s["x2"], 8)
                            else:
                                g.wait_ge(s["d3e"], 64)
                            for hb in range(32):
                                ghb = 32 * tc + hb
                                if ghb >= 2:
                                    g.wait_ge(s["gted"], 2 * (ghb - 1))
                                gp = "guw0" if ghb % 2 == 0 else "guw1"
                                g.dma_start(
                                    out=guw[ghb % 2][:, 0:4096].rearrange("p (d c) -> p d c", d=16),
                                    in_=wgT[hb].rearrange("d p c -> p d c"),
                                ).then_inc(s[gp], 16)
                                g.dma_start(
                                    out=guw[ghb % 2][:, 4096:8192].rearrange("p (d c) -> p d c", d=16),
                                    in_=wuT[hb].rearrange("d p c -> p d c"),
                                ).then_inc(s[gp], 16)

                        @bB.scalar
                        def _(sc, tc=tc):
                            for k in range(64):
                                tj = 64 * tc + k
                                sc.wait_ge(s["tp"], tj + 1)
                                tt, j = k // 16, k % 16
                                sc.copy(
                                    x2T[:, 512 * j + 128 * tt:512 * j + 128 * tt + 128],
                                    tp_ps[tj % 4][:, :],
                                ).then_inc(s["tpe"], 1)
                            for gct0 in range(64):
                                gct = 64 * tc + gct0
                                sc.wait_ge(s["g"], gct + 1)
                                if gct >= 2:
                                    sc.wait_ge(s["gted"], gct - 1)
                                sc.activation(out=gsil[gct % 2][:, :], in_=gA[gct % 2][:, :],
                                              func=AF.Silu).then_inc(s["sil"], 1)
                            if tc == 1:
                                for tt in range(4):
                                    sc.wait_ge(s["var"], 8 + tt + 1)
                                    rstd_emit(sc)

                        @bB.vector
                        def _(v, tc=tc):
                            for gct0 in range(64):
                                gct = 64 * tc + gct0
                                H = gct0
                                v.wait_ge(s["sil"], gct + 1)
                                v.wait_ge(s["u"], gct + 1)
                                v.tensor_tensor(
                                    gatedT[:, 512 * H:512 * (H + 1)],
                                    gsil[gct % 2][:, :], uA[gct % 2][:, :], ALU.mult,
                                ).then_inc(s["gted"], 1)
                            if tc == 1:
                                v.wait_ge(s["d3e"], 64)
                                for tt in range(4):
                                    gt = tt
                                    ln_stats_emit(v, x2[:, 2048 * gt:2048 * (gt + 1)])
                                    v.wait_ge(s["rstd"], 8 + tt + 1)
                                    if gt >= 2:
                                        v.wait_ge(s["outd0" if gt % 2 == 0 else "outd1"],
                                                  16 * (gt // 2))
                                    v.tensor_scalar(
                                        out_sb[gt % 2][:, :], x2[:, 2048 * gt:2048 * (gt + 1)],
                                        stats[:, 2:3], stats[:, 7:8], ALU.subtract, ALU.mult,
                                    ).then_inc(s["ob"], 1)

                        @bB.sync
                        def _(sy, tc=tc):
                            if tc == 1:
                                for tt in range(4):
                                    gt = tt
                                    sy.wait_ge(s["ob"], gt + 1)
                                    sy.dma_start(
                                        out=out[128 * gt:128 * (gt + 1), :],
                                        in_=out_sb[gt % 2][:, :],
                                    ).then_inc(s["outd0" if gt % 2 == 0 else "outd1"], 16)

                        @bB.tensor
                        def _(t, tc=tc):
                            for k in range(64):
                                tj = 64 * tc + k
                                tt, j = k // 16, k % 16
                                gt = 4 * tc + tt
                                t.wait_ge(s["x2"], gt + 1)
                                if tj >= 4:
                                    t.wait_ge(s["tpe"], tj - 3)
                                t.transpose(tp_ps[tj % 4][:, :],
                                            x2[:, 2048 * gt + 128 * j:2048 * gt + 128 * (j + 1)],
                                            identf_sb[:, :]).then_inc(s["tp"], 1)
                            t.wait_ge(s["tpe"], 64 * (tc + 1))
                            for hb in range(32):
                                ghb = 32 * tc + hb
                                t.wait_ge(s["guw0" if ghb % 2 == 0 else "guw1"],
                                          32 * (ghb // 2 + 1))
                                for ht in range(2):
                                    gct = 64 * tc + 2 * hb + ht
                                    if gct >= 2:
                                        t.wait_ge(s["sil"], gct - 1)
                                    for d in range(16):
                                        ins = t.matmul(
                                            gA[gct % 2][:, :],
                                            guw[ghb % 2][:, 256 * d + 128 * ht:256 * d + 128 * (ht + 1)],
                                            x2T[:, 512 * d:512 * (d + 1)],
                                            start=(d == 0), stop=(d == 15),
                                        )
                                    ins.then_inc(s["g"], 1)
                                    if gct >= 2:
                                        t.wait_ge(s["gted"], gct - 1)
                                    for d in range(16):
                                        ins = t.matmul(
                                            uA[gct % 2][:, :],
                                            guw[ghb % 2][:, 4096 + 256 * d + 128 * ht:
                                                          4096 + 256 * d + 128 * (ht + 1)],
                                            x2T[:, 512 * d:512 * (d + 1)],
                                            start=(d == 0), stop=(d == 15),
                                        )
                                    ins.then_inc(s["u"], 1)

                    with (
                        nc.sbuf_tensor(f"wdT0_{tc}", [128, 8 * 1024], BF16) as wdT0,
                        nc.sbuf_tensor(f"wdT1_{tc}", [128, 8 * 1024], BF16) as wdT1,
                        nc.sbuf_tensor(f"p3_{tc}", [128, 4 * 1024], F32) as p3,
                        nc.psum_tensor(f"d3_{tc}", [128, 2048], F32) as d3,
                        nc.Block() as bC,
                    ):
                        wdT_sb = [wdT0, wdT1]

                        @bC.gpsimd
                        def _(g, tc=tc):
                            g.wait_ge(s["wcd"], 16 * 16)
                            for dh in range(2):
                                for ho in range(8):
                                    gb = 16 * tc + 8 * dh + ho
                                    if gb >= 2:
                                        g.wait_ge(s["d3"], 4 * (gb - 1))
                                    g.dma_start(
                                        out=wdT_sb[gb % 2][:, :].rearrange("p (k c) -> p k c", k=8),
                                        in_=wdT_b[dh, ho].rearrange("k p c -> p k c"),
                                    ).then_inc(s["wd0" if gb % 2 == 0 else "wd1"], 16)

                        @bC.scalar
                        def _(sc, tc=tc):
                            if tc == 1:
                                for tt in range(4):
                                    sc.wait_ge(s["var"], 12 + tt + 1)
                                    rstd_emit(sc)

                        @bC.vector
                        def _(v, tc=tc):
                            for dh in range(2):
                                for ho in range(8):
                                    for tt in range(4):
                                        gt = 4 * tc + tt
                                        dct = 64 * tc + 32 * dh + 4 * ho + tt
                                        v.wait_ge(s["d3"], dct + 1)
                                        dps = d3[:, 1024 * (tt % 2):1024 * (tt % 2 + 1)]
                                        x2sl = x2[:, 2048 * gt + 1024 * dh:
                                                  2048 * gt + 1024 * (dh + 1)]
                                        p3sl = p3[:, 1024 * tt:1024 * (tt + 1)]
                                        if ho == 0:
                                            v.tensor_add(p3sl, dps, x2sl).then_inc(s["d3e"], 1)
                                        elif ho < 7:
                                            v.tensor_add(p3sl, dps, p3sl).then_inc(s["d3e"], 1)
                                        else:
                                            v.tensor_add(x2sl, dps, p3sl).then_inc(s["d3e"], 1)
                            if tc == 1:
                                v.wait_ge(s["d3e"], 128)
                                for tt in range(4):
                                    gt = 4 + tt
                                    ln_stats_emit(v, x2[:, 2048 * gt:2048 * (gt + 1)])
                                    v.wait_ge(s["rstd"], 12 + tt + 1)
                                    v.wait_ge(s["outd0" if gt % 2 == 0 else "outd1"],
                                              16 * (gt // 2))
                                    v.tensor_scalar(
                                        out_sb[gt % 2][:, :], x2[:, 2048 * gt:2048 * (gt + 1)],
                                        stats[:, 2:3], stats[:, 7:8], ALU.subtract, ALU.mult,
                                    ).then_inc(s["ob"], 1)

                        @bC.sync
                        def _(sy, tc=tc):
                            if tc == 1:
                                for tt in range(4):
                                    gt = 4 + tt
                                    sy.wait_ge(s["ob"], gt + 1)
                                    sy.dma_start(
                                        out=out[128 * gt:128 * (gt + 1), :],
                                        in_=out_sb[gt % 2][:, :],
                                    ).then_inc(s["outd0" if gt % 2 == 0 else "outd1"], 16)
                                sy.wait_ge(s["outd0"], 16 * 4)
                                sy.wait_ge(s["outd1"], 16 * 4)

                        @bC.tensor
                        def _(t, tc=tc):
                            for dh in range(2):
                                for ho in range(8):
                                    gb = 16 * tc + 8 * dh + ho
                                    t.wait_ge(s["wd0" if gb % 2 == 0 else "wd1"],
                                              16 * (gb // 2 + 1))
                                    if dh == 0 and ho == 0:
                                        t.wait_ge(s["gted"], 64 * (tc + 1))
                                    for tt in range(4):
                                        dct = 64 * tc + 32 * dh + 4 * ho + tt
                                        if dct >= 2:
                                            t.wait_ge(s["d3e"], dct - 1)
                                        for hc in range(8):
                                            H = 8 * ho + hc
                                            for j in range(2):
                                                ins = t.matmul(
                                                    d3[:, 1024 * (tt % 2) + 512 * j:
                                                       1024 * (tt % 2) + 512 * (j + 1)],
                                                    gatedT[:, 512 * H + 128 * tt:
                                                           512 * H + 128 * (tt + 1)],
                                                    wdT_sb[gb % 2][:, 1024 * hc + 512 * j:
                                                                   1024 * hc + 512 * (j + 1)],
                                                    start=(hc == 0), stop=(hc == 7),
                                                )
                                        ins.then_inc(s["d3"], 1)

    return nc


_NC_CACHE = None


def _get_nc():
    global _NC_CACHE
    if _NC_CACHE is None:
        _NC_CACHE = build_nc()
    return _NC_CACHE


def kernel(L0_attn_x, L0_attn_w_qkv, L0_attn_w_out, L0_ffn_w_gate,
           L0_ffn_w_up, L0_ffn_w_down):
    x = np.ascontiguousarray(np.asarray(L0_attn_x, dtype=np.float32).reshape(SEQ, DM))
    w_qkv = np.asarray(L0_attn_w_qkv, dtype=np.float32)
    w_out = np.asarray(L0_attn_w_out, dtype=np.float32)
    w_g = np.asarray(L0_ffn_w_gate, dtype=np.float32)
    w_u = np.asarray(L0_ffn_w_up, dtype=np.float32)
    w_d = np.asarray(L0_ffn_w_down, dtype=np.float32)

    xq = np.ascontiguousarray(x.reshape(NWIN, WIN, 16, 128).transpose(0, 2, 3, 1))
    wo = np.ascontiguousarray(
        w_out.reshape(16, 128, 2048).transpose(1, 0, 2).reshape(128, 16 * 2048))
    wgT = np.ascontiguousarray(w_g.reshape(32, 256, 16, 128).transpose(0, 2, 3, 1))
    wuT = np.ascontiguousarray(w_u.reshape(32, 256, 16, 128).transpose(0, 2, 3, 1))
    wdT = np.ascontiguousarray(
        w_d.reshape(2, 1024, 8, 8, 128).transpose(0, 2, 3, 4, 1))

    in_maps = []
    for c in range(NC_):
        h0, h1 = 2 * c, 2 * c + 1
        blocks = [w_qkv[:, h0 * 128:(h0 + 1) * 128],
                  w_qkv[:, h1 * 128:(h1 + 1) * 128],
                  w_qkv[:, 2048 + h0 * 128:2048 + (h0 + 1) * 128],
                  w_qkv[:, 2048 + h1 * 128:2048 + (h1 + 1) * 128],
                  w_qkv[:, 4096 + h0 * 128:4096 + (h0 + 1) * 128],
                  w_qkv[:, 4096 + h1 * 128:4096 + (h1 + 1) * 128]]
        wq_c = np.stack(blocks, axis=1)
        wq_c = np.ascontiguousarray(
            wq_c.reshape(16, 128, 6, 128).transpose(1, 0, 2, 3).reshape(128, 16 * 768))
        in_maps.append({
            "xq": xq, "wq": wq_c, "wo": wo, "wgT": wgT, "wuT": wuT,
            "wdT": wdT, "xres": np.ascontiguousarray(x[TOKS * c:TOKS * (c + 1)]),
        })

    nc = _get_nc()
    res = run_bass_kernel_spmd(nc, in_maps, core_ids=list(range(NC_)))
    out = np.concatenate([r["out"] for r in res.results], axis=0)
    return out.reshape(1, SEQ, DM)
